# version 63
# baseline (speedup 1.0000x reference)
import numpy as np
import ml_dtypes
from contextlib import ExitStack

import concourse.bass as bass
import concourse.mybir as mybir
from concourse.bass_utils import run_bass_kernel_spmd

F32 = mybir.dt.float32
BF16 = mybir.dt.bfloat16
I16 = mybir.dt.int16
AX = mybir.AxisListType
OP = mybir.AluOpType
AF = mybir.ActivationFunctionType

D = 1024
T = 2048
NCH = 16
H = 16
DI = 2816
LN_EPS = 1e-5
FEPS = 1e-4
DEN_EPS = 8e-4
THR = [0.25, 0.75, 1.25, 1.75, 2.5, 3.5, 5.0]
C3 = 1.0 / 3.0

ENGINES = ("sync", "gpsimd", "tensor", "vector", "scalar")
SEMS = ("ldx", "ldw", "out", "pe", "dve", "act", "ldp0", "ldp1", "lds",
        "pool", "wout", "wl2", "wl3", "wl4", "wl5", "wl6", "wl7",
        "wl8", "wl9", "wlA", "wlB", "wlC", "wlD", "wlE", "wlF")
STOP_AFTER = None
UPTO = "zz"


class Prog:

    ENG_SEM = {"vector": "dve", "scalar": "act", "gpsimd": "pool"}

    def __init__(self):
        self.ops = {e: [] for e in ENGINES}
        self.n = {s: 0 for s in SEMS}
        self.sems = {}
        self.last_wait = {e: {} for e in ENGINES}

    def tok(self, s):
        return (s, self.n[s])

    def emit(self, eng, fn, waits=(), inc=None, k=1):
        wl = tuple((w, self.n[w]) if isinstance(w, str) else w for w in waits)
        es = self.ENG_SEM.get(eng)
        if es is not None:
            wl = wl + ((es, self.n[es]),)
        tok = None
        if inc is not None:
            self.n[inc] += k
            tok = (inc, self.n[inc])
        self.ops[eng].append((wl, fn, inc, k))
        return tok

    def join(self):
        toks = tuple((s, self.n[s]) for s in SEMS)
        for e in ENGINES:
            self.ops[e].append((toks, None, None, 0))

    def flush(self, block):
        for e in ENGINES:
            lst = self.ops[e]
            self.ops[e] = []
            if not lst:
                continue

            def section(eng, lst=lst, e=e):
                lw = self.last_wait[e]
                for (wl, fn, inc, k) in lst:
                    for (s, c) in wl:
                        if c > 0 and lw.get(s, 0) < c:
                            eng.wait_ge(self.sems[s], c)
                            lw[s] = c
                    if fn is not None:
                        inst = fn(eng)
                        if inc is not None:
                            inst.then_inc(self.sems[inc], k)

            getattr(block, e)(section)


def build_program():
    nc = bass.Bass("TRN2", target_bir_lowering=False)

    d_xm = nc.dram_tensor("xm", [T, D], F32, kind="ExternalInput").ap()
    d_xp = nc.dram_tensor("xp", [T, D], F32, kind="ExternalInput").ap()
    d_mc = nc.dram_tensor("mc", [128, 1], F32, kind="ExternalInput").ap()
    d_wqkvT = nc.dram_tensor("wqkvT", [D, 3 * D], F32, kind="ExternalInput").ap()
    d_bqk_cols = nc.dram_tensor("bqk_cols", [128, 16], F32, kind="ExternalInput").ap()
    d_bqkk_row = nc.dram_tensor("bqkk_row", [1, D], F32, kind="ExternalInput").ap()
    d_bqv_row = nc.dram_tensor("bqv_row", [1, D], F32, kind="ExternalInput").ap()
    d_woutT = nc.dram_tensor("woutT", [D, D], F32, kind="ExternalInput").ap()
    d_bo_row = nc.dram_tensor("bo_row", [1, D], F32, kind="ExternalInput").ap()
    d_gamma = nc.dram_tensor("gamma_t", [128, D], F32, kind="ExternalInput").ap()
    d_beta = nc.dram_tensor("beta_t", [128, D], F32, kind="ExternalInput").ap()
    d_wg = nc.dram_tensor("wg", [DI, D], F32, kind="ExternalInput").ap()
    d_wu = nc.dram_tensor("wu", [DI, D], F32, kind="ExternalInput").ap()
    d_wd = nc.dram_tensor("wd", [D, DI], F32, kind="ExternalInput").ap()
    d_id = nc.dram_tensor("ident", [128, 128], BF16, kind="ExternalInput").ap()
    d_mtri = nc.dram_tensor("mtri", [128, 512], BF16, kind="ExternalInput").ap()
    d_ones = nc.dram_tensor("onesf", [1, 128], F32, kind="ExternalInput").ap()
    d_y = nc.dram_tensor("y", [T, D], F32, kind="ExternalOutput").ap()
    d_wq = nc.dram_tensor("wqscr", [60, 128, 1408], BF16, kind="Internal").ap()
    d_xs = nc.dram_tensor("xscr", [T, D], F32, kind="Internal").ap()

    p = Prog()
    dve, act, pe = "dve", "act", "pe"

    with ExitStack() as root:
        for s in SEMS:
            p.sems[s] = root.enter_context(nc.semaphore(s))
        S0 = root.enter_context(nc.psum_tensor("S0", [128, 512], F32))
        S1 = root.enter_context(nc.psum_tensor("S1", [128, 512], F32))
        G = [root.enter_context(nc.psum_tensor(f"G{i}", [128, 512], F32))
             for i in range(5)]
        TP = root.enter_context(nc.psum_tensor("TP", [128, 1024], BF16))
        block = root.enter_context(nc.Block())

        Id = root.enter_context(nc.sbuf_tensor("Id", [128, 128], BF16))
        Mc = root.enter_context(nc.sbuf_tensor("Mc", [128, 1], F32))

        p.emit("sync", lambda e: e.dma_start(out=Id[:, :], in_=d_id), inc="ldx", k=16)
        p.emit("sync", lambda e: e.dma_start(out=Mc[:, :], in_=d_mc), inc="ldx", k=16)

        def quant(src_f32, F, dst_bf16, FA, B1, B2, MB, SBuf, T1, T2, SBB, w0=(),
                  dst_w=("pe",), xeng="vector", B3=None, aeng="vector"):
            nb = F // 16
            xinc = dve if xeng == "vector" else "pool"
            src3 = src_f32.rearrange("p (b s) -> p b s", s=16)
            p.emit("vector", lambda e: e.tensor_reduce(out=MB[:, 0:nb], in_=src3,
                   axis=AX.X, op=OP.max, apply_absolute_value=True),
                   waits=tuple(w0), inc=dve)
            p.emit("vector", lambda e: e.tensor_scalar_mul(T1[:, 0:nb], MB[:, 0:nb], 0.5),
                   inc=dve)
            p.emit("vector", lambda e: e.tensor_scalar_mul(T2[:, 0:nb], T1[:, 0:nb], C3),
                   inc=dve)
            p.emit("vector", lambda e: e.scalar_tensor_tensor(out=MB[:, 0:nb],
                   in0=T2[:, 0:nb], scalar=-2.0, in1=T1[:, 0:nb], op0=OP.mult,
                   op1=OP.add), inc=dve)
            p.emit("vector", lambda e: e.scalar_tensor_tensor(out=T1[:, 0:nb],
                   in0=T2[:, 0:nb], scalar=-1.0, in1=MB[:, 0:nb], op0=OP.mult,
                   op1=OP.add), inc=dve)
            p.emit("vector", lambda e: e.scalar_tensor_tensor(out=SBuf[:, 0:nb],
                   in0=T1[:, 0:nb], scalar=C3, in1=T2[:, 0:nb], op0=OP.mult,
                   op1=OP.add), inc=dve)
            p.emit("vector", lambda e: e.tensor_scalar_max(SBuf[:, 0:nb], SBuf[:, 0:nb],
                   1e-12), inc=dve)
            p.emit("vector", lambda e: e.reciprocal(out=T2[:, 0:nb], in_=SBuf[:, 0:nb]),
                   inc=dve)
            p.emit("vector", lambda e: e.tensor_copy(out=SBB[:, 0:nb], in_=T2[:, 0:nb]),
                   inc=dve)
            p.emit("vector", lambda e: e.tensor_copy(out=SBB[:, 176:176 + nb],
                   in_=SBuf[:, 0:nb]), inc=dve)
            if aeng == "scalar":
                tok_abs = p.emit("scalar", lambda e: e.activation(out=FA[:, 0:F],
                       in_=src_f32, func=AF.Abs), waits=tuple(w0), inc=act)
            else:
                p.emit("vector", lambda e: e.tensor_scalar(out=FA[:, 0:F],
                       in0=src_f32, scalar1=0.0, scalar2=None, op0=OP.max),
                       waits=tuple(w0), inc=dve)
                p.emit("vector", lambda e: e.tensor_scalar(out=B1[:, 0:F],
                       in0=src_f32,
                       scalar1=-1.0, scalar2=0.0, op0=OP.mult, op1=OP.max), inc=dve)
                tok_abs = p.emit("vector", lambda e: e.tensor_tensor(out=FA[:, 0:F],
                       in0=FA[:, 0:F], in1=B1[:, 0:F], op=OP.add), inc=dve)
            rb = SBB[:, 0:nb].unsqueeze(2).broadcast_to((128, nb, 16))
            tok_z = p.emit(xeng, lambda e: e.tensor_tensor(
                out=FA[:, 0:F].rearrange("p (b s) -> p b s", s=16),
                in0=FA[:, 0:F].rearrange("p (b s) -> p b s", s=16), in1=rb,
                op=OP.mult), waits=(tok_abs, dve), inc=xinc)

            def cmp(i, dst):
                return lambda e: e.tensor_single_scalar(out=dst, in_=FA[:, 0:F],
                                                        scalar=THR[i], op=OP.is_gt)

            if B3 is not None and xeng == "gpsimd":
                padd = lambda dst, s1: (lambda e: e.tensor_tensor(
                    out=dst[:, 0:F], in0=dst[:, 0:F], in1=s1[:, 0:F], op=OP.add))
                p.emit("vector", cmp(0, B1[:, 0:F]), waits=(tok_z,), inc=dve)
                p.emit("vector", cmp(1, B2[:, 0:F]), inc=dve)
                a1 = p.emit("gpsimd", padd(B1, B2), waits=(dve,), inc="pool")
                p.emit("vector", cmp(2, B3[:, 0:F]), inc=dve)
                a2 = p.emit("gpsimd", padd(B1, B3), waits=(dve,), inc="pool")
                p.emit("vector", cmp(3, B2[:, 0:F]), waits=(a1,), inc=dve)
                a3 = p.emit("gpsimd", padd(B1, B2), waits=(dve,), inc="pool")
                p.emit("vector", cmp(4, B3[:, 0:F]), waits=(a2,), inc=dve)
                p.emit("vector", cmp(5, B2[:, 0:F]), waits=(a3,), inc=dve)
                a4 = p.emit("gpsimd", padd(B3, B2), waits=(dve,), inc="pool")
                p.emit("vector", cmp(6, B2[:, 0:F]), waits=(a4,), inc=dve)
                p.emit("vector", lambda e: e.scalar_tensor_tensor(out=B1[:, 0:F],
                       in0=B1[:, 0:F], scalar=0.5, in1=B3[:, 0:F], op0=OP.mult,
                       op1=OP.add), waits=(a3, a4), inc=dve)
                p.emit("vector", lambda e: e.scalar_tensor_tensor(out=B1[:, 0:F],
                       in0=B2[:, 0:F], scalar=2.0, in1=B1[:, 0:F], op0=OP.mult,
                       op1=OP.add), inc=dve)
            else:
                p.emit("vector", cmp(0, B1[:, 0:F]), waits=(tok_z,), inc=dve)
                for i in (1, 2, 3):
                    p.emit("vector", cmp(i, B2[:, 0:F]), inc=dve)
                    p.emit("vector", lambda e: e.tensor_tensor(out=B1[:, 0:F],
                           in0=B1[:, 0:F], in1=B2[:, 0:F], op=OP.add), inc=dve)
                p.emit("vector", lambda e: e.tensor_scalar_mul(B1[:, 0:F],
                       B1[:, 0:F], 0.5), inc=dve)
                for i in (4, 5):
                    p.emit("vector", cmp(i, B2[:, 0:F]), inc=dve)
                    p.emit("vector", lambda e: e.tensor_tensor(out=B1[:, 0:F],
                           in0=B1[:, 0:F], in1=B2[:, 0:F], op=OP.add), inc=dve)
                p.emit("vector", cmp(6, B2[:, 0:F]), inc=dve)
                p.emit("vector", lambda e: e.scalar_tensor_tensor(out=B1[:, 0:F],
                       in0=B2[:, 0:F], scalar=2.0, in1=B1[:, 0:F], op0=OP.mult,
                       op1=OP.add), inc=dve)
            if aeng == "scalar":
                tok_sgn = p.emit("scalar", lambda e: e.activation(out=FA[:, 0:F],
                       in_=src_f32, func=AF.Sign), waits=(dve,), inc=act)
            else:
                p.emit("vector", lambda e: e.tensor_single_scalar(out=FA[:, 0:F],
                       in_=src_f32, scalar=0.0, op=OP.is_gt), inc=dve)
                tok_sgn = p.emit("vector", lambda e: e.tensor_scalar(out=FA[:, 0:F],
                       in0=FA[:, 0:F], scalar1=2.0, scalar2=-1.0, op0=OP.mult,
                       op1=OP.add), inc=dve)
            sb_b = SBB[:, 176:176 + nb].unsqueeze(2).broadcast_to((128, nb, 16))
            tok_b2 = p.emit(xeng, lambda e: e.tensor_tensor(
                out=B2[:, 0:F].rearrange("p (b s) -> p b s", s=16),
                in0=B1[:, 0:F].rearrange("p (b s) -> p b s", s=16), in1=sb_b,
                op=OP.mult), waits=(dve,), inc=xinc)
            tok_d = p.emit("vector", lambda e: e.tensor_tensor(out=dst_bf16,
                   in0=B2[:, 0:F], in1=FA[:, 0:F], op=OP.mult),
                   waits=(tok_sgn, tok_b2) + tuple(dst_w), inc=dve)
            return tok_d, tok_sgn

        quant_g = lambda *a, **k: quant(*a, xeng="gpsimd", aeng="scalar", **k)[0]

        s12 = ExitStack()
        sb = lambda n, shp, dt: s12.enter_context(nc.sbuf_tensor(n, shp, dt))
        Mtri = sb("Mtri", [128, 512], BF16)
        OnesF = sb("OnesF", [1, 128], F32)
        BqkC = sb("BqkC", [128, 16], F32)
        BqkkR = sb("BqkkR", [1, D], F32)
        BqvR = sb("BqvR", [1, D], F32)
        BoR = sb("BoR", [1, D], F32)
        Gam = sb("Gam", [128, D], F32)
        Bet = sb("Bet", [128, D], F32)
        Wq = sb("Wq", [128, 8 * 3072], BF16)
        Wo = sb("Wo", [128, 8 * 1024], BF16)
        XN = sb("XN", [128, 2 * 1024], BF16)
        XNT = sb("XNT", [128, 8 * 512], BF16)
        QT = sb("QT", [128, 8 * 512], BF16)
        KT = sb("KT", [128, 8 * 512], BF16)
        KTM = sb("KTM", [128, 1024], BF16)
        Vp = sb("Vp", [128, 1040], BF16)
        STm = sb("STm", [128, 2048], BF16)
        ATT = sb("ATT", [128, 1024], BF16)
        KVs = sb("KVs", [128, 520], BF16)
        SST = sb("SST", [128, 520], F32)
        F1 = sb("F1", [128, 1024], F32)
        F2 = sb("F2", [128, 1024], F32)
        F3a = sb("F3a", [128, 512], F32)
        F3b = F2[:, 512:1024]
        DC = sb("DC", [128, 16], F32)
        XC = sb("XC", [128, 6 * 1024], F32)
        MCOL = sb("MCOL", [128, 1], F32)
        VCOL = sb("VCOL", [128, 1], F32)
        VC2 = sb("VC2", [128, 1], F32)
        T0C = sb("T0C", [128, 1], F32)
        T1C = sb("T1C", [128, 1], F32)
        mcol = MCOL[:, 0:1]; vcol = VCOL[:, 0:1]; vc2 = VC2[:, 0:1]
        t0c = T0C[:, 0:1]; t1c = T1C[:, 0:1]

        for (dst, src) in ((Mtri, d_mtri), (OnesF, d_ones), (BqkC, d_bqk_cols),
                           (BqkkR, d_bqkk_row), (BqvR, d_bqv_row), (BoR, d_bo_row),
                           (Gam, d_gamma), (Bet, d_beta)):
            p.emit("sync", lambda e, a=dst, b=src: e.dma_start(out=a[:, :], in_=b),
                   inc="ldx", k=16)
        for j in range(8):
            p.emit("gpsimd", lambda e, j=j: e.dma_start(
                out=Wq[:, j * 3072:(j + 1) * 3072],
                in_=d_wqkvT[128 * j:128 * (j + 1), :]), inc="ldw", k=16)
        for j in range(8):
            p.emit("gpsimd", lambda e, j=j: e.dma_start(
                out=Wo[:, j * 1024:(j + 1) * 1024],
                in_=d_woutT[128 * j:128 * (j + 1), :]), inc="ldw", k=16)
        TOK_CONST = (p.tok("ldx"), p.tok("ldw"))
        NBqkC = sb("NBqkC", [128, 16], F32)
        p.emit("vector", lambda e: e.tensor_scalar_mul(NBqkC[:, :], BqkC[:, :], -1.0),
               waits=(TOK_CONST[0],), inc=dve)

        p1_sq = [None]

        def ln_to(src_ap, dst_ap, extra=()):
            p.emit("vector", lambda e: e.reduce_sum(out=mcol, in_=src_ap, axis=AX.X),
                   waits=tuple(extra), inc=dve)
            p.emit("vector", lambda e: e.tensor_scalar_mul(mcol, mcol, 1.0 / D),
                   inc=dve)
            p.emit("vector", lambda e: e.tensor_scalar(out=F1[:, :], in0=src_ap,
                   scalar1=mcol, scalar2=None, op0=OP.subtract),
                   waits=(p1_sq[0],) if p1_sq[0] else (), inc=dve)
            p1_sq[0] = p.emit("scalar", lambda e: e.activation(out=F2[:, :],
                   in_=F1[:, :], func=AF.Square, accum_out=vcol),
                   waits=(dve,), inc=act)
            p.emit("vector", lambda e: e.tensor_scalar(out=vcol, in0=vcol,
                   scalar1=1.0 / D, scalar2=LN_EPS, op0=OP.mult, op1=OP.add),
                   waits=(act,), inc=dve)
            p.emit("scalar", lambda e: e.activation(out=vc2, in_=vcol, func=AF.Ln),
                   waits=(dve,), inc=act)
            p.emit("scalar", lambda e: e.activation(out=vc2, in_=vc2, func=AF.Exp,
                   scale=-0.5), inc=act)
            return p.emit("vector", lambda e: e.tensor_scalar(out=dst_ap,
                   in0=F1[:, :], scalar1=vc2, scalar2=None, op0=OP.mult),
                   waits=(act,), inc=dve)

        tp_free = [None]

        def transposes(src_slices, dst3, n, src_tok=None):
            for j in range(n):
                w = ()
                if j == 0:
                    w = (src_tok,) if src_tok else (pe, dve, act)
                    if tp_free[0]:
                        w = w + (tp_free[0],)
                p.emit("tensor", lambda e, j=j: e.transpose(
                    TP[:, 128 * j:128 * (j + 1)], src_slices(j), Id[:, :]),
                    waits=w, inc=pe)
            tp_free[0] = p.emit("scalar", lambda e: e.copy(
                out=dst3, in_=TP[:, 0:128 * n].rearrange("p (j t) -> p j t", t=128)),
                waits=(pe,), inc=act)
            return tp_free[0]

        def feature_evac(bank, dst_ap, bias_col=None, nbias_col=None, pe_tok=None):
            w0 = (pe_tok,) if pe_tok else (pe,)
            w0 = w0 + (dve,)
            if bias_col is None:
                p.emit("scalar", lambda e: e.activation(out=F3a[:, :], in_=bank[:, :],
                       func=AF.Relu, scale=-1.0), waits=w0, inc=act)
                tok_r2 = p.emit("scalar", lambda e: e.activation(out=F3b,
                       in_=bank[:, :], func=AF.Relu), inc=act)
            else:
                p.emit("scalar", lambda e: e.activation(out=F3a[:, :], in_=bank[:, :],
                       func=AF.Relu, scale=-1.0, bias=nbias_col),
                       waits=w0, inc=act)
                tok_r2 = p.emit("scalar", lambda e: e.activation(out=F3b,
                       in_=bank[:, :], func=AF.Relu, bias=bias_col), inc=act)
            p.emit("scalar", lambda e: e.activation(out=F2[:, 0:512], in_=F3a[:, :],
                   func=AF.Exp, scale=-1.0), inc=act)
            return p.emit("vector", lambda e: e.scalar_tensor_tensor(out=dst_ap,
                   in0=F3b, scalar=FEPS, in1=F2[:, 0:512], op0=OP.add, op1=OP.add),
                   waits=(act,), inc=dve)

        def state_slice(h):
            Sb = S0 if h < 8 else S1
            pr = (h // 2) % 4
            return Sb[64 * (h % 2):64 * (h % 2) + 64, 65 * pr:65 * pr + 65]

        def kvs_slice(h):
            off = 260 * (h // 8) + 65 * ((h // 2) % 4)
            return KVs[64 * (h % 2):64 * (h % 2) + 64, off:off + 65]

        s1x = ExitStack()
        sb1 = lambda n, shp, dt: s1x.enter_context(nc.sbuf_tensor(n, shp, dt))
        RW = sb1("RW", [128, 2 * 2048], F32)
        WQS = sb1("WQS", [128, 2048], BF16)
        W_FA = sb1("wFA", [128, 2048], BF16)
        W_B1 = sb1("wB1", [128, 2048], BF16)
        W_B2 = sb1("wB2", [128, 2048], BF16)
        W_MB = sb1("wMB", [128, 128], F32)
        W_SB = sb1("wSB", [128, 128], F32)
        W_T1 = sb1("wT1", [128, 128], F32)
        W_T2 = sb1("wT2", [128, 128], F32)
        SBBp = sb1("SBBp", [128, 304], BF16)
        W_DQ = sb1("wDQ", [128, 2 * 2048], BF16)
        

        p.emit("vector", lambda e: e.memset(SST[:, :], 0.0), inc=dve)
        p.emit("vector", lambda e: e.tensor_copy(
            out=Vp[:, :].rearrange("p (h f) -> p h f", f=65)[:, :, 64:65],
            in_=Mc[:, 0:1].unsqueeze(1).broadcast_to((128, 16, 1))),
            waits=(TOK_CONST[0],), inc=dve)

        NWP = 38
        wpairs = []
        for it in range(11):
            wpairs.append(([(d_wg[256 * it:256 * it + 128, :], 0, 1024),
                            (d_wg[256 * it + 128:256 * it + 256, :], 1024, 1024)],
                           [(2 * it, 0, 1024), (2 * it + 1, 1024, 1024)], 2048, 16))
        for it in range(11):
            wpairs.append(([(d_wu[256 * it:256 * it + 128, :], 0, 1024),
                            (d_wu[256 * it + 128:256 * it + 256, :], 1024, 1024)],
                           [(22 + 2 * it, 0, 1024), (23 + 2 * it, 1024, 1024)],
                           2048, 16))
        for kt in range(8):
            for half in range(2):
                wpairs.append(([(d_wd[128 * kt:128 * (kt + 1),
                                      1408 * half:1408 * (half + 1)], 0, 1408)],
                               [(44 + 2 * kt + half, 0, 1408)], 1408, 11))
        w_load_tok = [None] * NWP
        w_q_tok = [None, None]
        w_out_tok = [None, None]
        w_last_out = [None]
        w_last_wqevac = [None]

        tk0 = None
        for (s_, off, wd_) in wpairs[0][0]:
            tk0 = p.emit("sync", lambda e, b=RW[:, off:off + wd_], s=s_:
                e.dma_start(out=b, in_=s), waits=() if tk0 is None else (tk0,),
                inc="lds", k=16)
        w_load_tok[0] = tk0

        w_tp_pe = [None, None]

        def wquant_q(i):
            (loads, stores, F, ntp) = wpairs[i]
            slot = 2048 * (i % 2)
            if i + 1 < NWP:
                nslot = 2048 * ((i + 1) % 2)
                w = (w_load_tok[i],)
                if w_q_tok[(i + 1) % 2]:
                    w = w + (w_q_tok[(i + 1) % 2],)
                if w_tp_pe[(i + 1) % 2]:
                    w = w + (w_tp_pe[(i + 1) % 2],)
                tk = None
                for (s_, off, wd_) in wpairs[i + 1][0]:
                    tk = p.emit("sync", lambda e,
                        b=RW[:, nslot + off:nslot + off + wd_], s=s_:
                        e.dma_start(out=b, in_=s),
                        waits=w if tk is None else (tk,), inc="lds", k=16)
                w_load_tok[i + 1] = tk
            wq_w0 = (w_load_tok[i],)
            if w_q_tok[(i + 1) % 2]:
                wq_w0 = wq_w0 + (w_q_tok[(i + 1) % 2],)
            dstv = W_DQ[:, 2048 * (i % 2):2048 * (i % 2) + 2048]
            dw = (w_tp_pe[i % 2],) if w_tp_pe[i % 2] else ("pe",)
            w_q_tok[i % 2] = quant_g(RW[:, slot:slot + F], F,
                                   dstv[:, 0:F], W_FA, W_B1, W_B2,
                                   W_MB, W_SB, W_T1, W_T2, SBBp,
                                   w0=wq_w0, dst_w=dw)

        def wquant_tp(i):
            (loads, stores, F, ntp) = wpairs[i]
            slot = 2048 * (i % 2)
            dstv = W_DQ[:, 2048 * (i % 2):2048 * (i % 2) + 2048]
            wq = WQS[:, 0:2048]
            done = 0
            evac_tok = None
            while done < ntp:
                nb_t = min(8, ntp - done)
                for j in range(nb_t):
                    w = ()
                    if j == 0:
                        w = (w_q_tok[i % 2],)
                        if tp_free[0] is not None:
                            w = w + (tp_free[0],)
                        if done == 0 and w_last_out[0] is not None:
                            w = w + (w_last_out[0],)
                    p.emit("tensor", lambda e, j=j, d0=done, dstv=dstv:
                        e.transpose(
                        TP[:, 128 * j:128 * (j + 1)],
                        dstv[:, 128 * (d0 + j):128 * (d0 + j + 1)], Id[:, :]),
                        waits=w, inc=pe)
                evac_tok = p.emit("scalar", lambda e, d0=done, n=nb_t, wq=wq:
                    e.copy(
                    out=wq[:, 128 * d0:128 * (d0 + n)],
                    in_=TP[:, 0:128 * n].rearrange("p (j t) -> p j t", t=128)),
                    waits=(pe,), inc=act)
                tp_free[0] = evac_tok
                done += nb_t
            w_last_wqevac[0] = evac_tok
            w_tp_pe[i % 2] = p.tok(pe)
            tk = None
            for (di, off, wd_) in stores:
                w = (evac_tok,) if tk is None else (tk,)
                if tk is None and w_last_out[0] is not None:
                    w = w + (w_last_out[0],)
                tk = p.emit("gpsimd", lambda e, di=di, off=off, wd_=wd_, wq=wq:
                    e.dma_start(out=d_wq[di, :, 0:wd_],
                                in_=wq[:, off:off + wd_]),
                    waits=w, inc="wout", k=16)
            w_out_tok[i % 2] = tk
            w_last_out[0] = tk

        NPREF = 0 if STOP_AFTER == "p0" else (1 if STOP_AFTER == "p1n1" else NCH)
        xp_tok = [None] * NCH
        xp_tok[0] = p.emit("sync", lambda e: e.dma_start(
            out=XC[:, 0:1024], in_=d_xp[0:128, :]), inc="ldp0", k=16)

        wq_i = [0]

        def emit_wquant_upto(limit):
            while wq_i[0] < min(limit, NWP):
                i = wq_i[0]
                if i >= 1:
                    wquant_tp(i - 1)
                wquant_q(i)
                wq_i[0] += 1

        bank_free = [None, None]
        bank_free_v = [None, None]
        sst_free = [None]
        tok_state_pe = [None]
        for c in range(NPREF):
            xpb = XC[:, 1024 * (c % 2):1024 * (c % 2) + 1024]
            if c + 1 < NCH:
                nbuf = XC[:, 1024 * ((c + 1) % 2):1024 * ((c + 1) % 2) + 1024]
                src = d_xp[128 * (c + 1):128 * (c + 2), :]
                xp_tok[c + 1] = p.emit("sync", lambda e, a=nbuf, b=src: e.dma_start(
                    out=a, in_=b), waits=(dve, xp_tok[c]), inc="ldp0" if (c + 1) % 2 == 0
                    else "ldp1", k=16)
            xnb = XN[:, 1024 * (c % 2):1024 * (c % 2) + 1024]
            tok_xn = ln_to(xpb, xnb, extra=(xp_tok[c],))
            tok_xnt = transposes(lambda j, xnb=xnb: xnb[:, 128 * j:128 * (j + 1)],
                       XNT[:, :].rearrange("p (j t) -> p j t", t=512)
                       [:, :, 128 * (c % 2):128 * (c % 2) + 128], 8,
                       src_tok=tok_xn)
            if NPREF == NCH:
                emit_wquant_upto(int((c + 0.34) * NWP / NCH))
            tok_kv_pe = [None, None]
            for kind in range(2):
                for n in range(2):
                    bank = G[n] if kind == 0 else G[3 + n]
                    bfree = bank_free if kind == 0 else bank_free_v
                    coff = 1024 * (kind + 1) + 512 * n
                    for j in range(8):
                        w = ()
                        if j == 0:
                            w = (tok_xnt,) + TOK_CONST
                            if bfree[n]:
                                w = w + (bfree[n],)
                        p.emit("tensor", lambda e, j=j, b=bank, co=coff, c=c:
                            e.matmul(
                            b[:, :], XNT[:, 512 * j + 128 * (c % 2):
                                         512 * j + 128 * (c % 2) + 128],
                            Wq[:, 3072 * j + co:3072 * j + co + 512],
                            start=(j == 0), stop=False),
                            waits=w, inc=pe)
                    brow = BqkkR if kind == 0 else BqvR
                    tok_kv_pe[n] = p.emit("tensor", lambda e, b=bank, r=brow, n=n:
                        e.matmul(b[:, :], OnesF[0:1, 0:128],
                                 r[0:1, 512 * n:512 * n + 512],
                                 start=False, stop=True), inc=pe)
                if kind == 0:
                    for n in range(2):
                        bank_free[n] = feature_evac(G[n],
                            KTM[:, 512 * n:512 * n + 512], pe_tok=tok_kv_pe[n])
                    if NPREF == NCH:
                        emit_wquant_upto(int((c + 0.67) * NWP / NCH))
                else:
                    for n in range(2):
                        dst = Vp[:, 520 * n:520 * n + 520].rearrange(
                            "p (h f) -> p h f", f=65)[:, :, 0:64]
                        bank_free_v[n] = p.emit("vector", lambda e, b=G[3 + n], d=dst:
                            e.tensor_scalar(
                            out=d, in0=b[:, :].rearrange("p (h f) -> p h f", f=64),
                            scalar1=Mc[:, 0:1], scalar2=None, op0=OP.mult),
                            waits=(tok_kv_pe[n],), inc=dve)
            for grp in range(2):
                first = True
                for idx in range(8):
                    h = 2 * idx + grp
                    w = ()
                    if first:
                        w = (bank_free[0], bank_free[1],
                             bank_free_v[0], bank_free_v[1])
                        if sst_free[0]:
                            w = w + (sst_free[0],)
                    p.emit("tensor", lambda e, h=h, idx=idx: e.matmul(
                        state_slice(h), KTM[:, 64 * h:64 * h + 64],
                        Vp[:, 65 * h:65 * h + 65], start=(idx in (0, 4)),
                        stop=(idx in (3, 7)),
                        tile_position=(0, 64 * (h % 2))),
                        waits=w, inc=pe)
                    first = False
            tok_state_pe[0] = p.tok(pe)
            p.emit("vector", lambda e: e.tensor_tensor(out=SST[:, 0:260],
                   in0=SST[:, 0:260], in1=S0[:, 0:260], op=OP.add),
                   waits=(tok_state_pe[0],), inc=dve)
            sst_free[0] = p.emit("vector", lambda e: e.tensor_tensor(
                   out=SST[:, 260:520],
                   in0=SST[:, 260:520], in1=S1[:, 0:260], op=OP.add), inc=dve)
            emit_wquant_upto((c + 1) * NWP // NCH if NPREF == NCH else 0)

        emit_wquant_upto(NWP)
        wquant_tp(NWP - 1)
        p.join()
        p.flush(block)
        s1x.close()
        s2q = ExitStack()
        sb2 = lambda n, shp, dt: s2q.enter_context(nc.sbuf_tensor(n, shp, dt))
        XQN = sb2("XQN", [128, 1024], F32)
        AO = sb2("AO", [128, 1024], F32)
        if STOP_AFTER in ("p0", "p1", "p1n1"):
            p.emit("vector", lambda e: e.tensor_copy(out=F1[:, 0:520], in_=SST[:, :]),
                   waits=(pe,), inc=dve)
            p.emit("gpsimd", lambda e: e.dma_start(out=d_y[0:128, :],
                   in_=F1[:, :]), waits=(dve, act, pe), inc="out", k=16)
            p.emit("gpsimd", lambda e: e.engine_nop(), waits=("out",))
            p.flush(block)
            s12.close()
            return nc

        p.emit("vector", lambda e: e.memset(
            Vp[:, :].rearrange("p (h f) -> p h f", f=65)[:, :, 64:65], 1.0), inc=dve)
        xm_tok = [None] * NCH
        xm_tok[0] = p.emit("sync", lambda e: e.dma_start(
            out=XC[:, 0:1024], in_=d_xm[0:128, :]), waits=("ldp0", "ldp1", dve),
            inc="ldp0", k=16)

        fe_tok = [None, None, None, None]
        fe_last = [None]
        xnt_tok = [None]

        def qkv_stage(g):
            for c4 in range(4):
                c = 4 * g + c4
                if c + 1 < NCH:
                    nbuf = XC[:, 1024 * ((c + 1) % 6):1024 * ((c + 1) % 6) + 1024]
                    src = d_xm[128 * (c + 1):128 * (c + 2), :]
                    xm_tok[c + 1] = p.emit("sync", lambda e, a=nbuf, b=src:
                        e.dma_start(out=a, in_=b), waits=(dve, "out", xm_tok[c]),
                        inc="ldp0" if (c + 1) % 2 == 0 else "ldp1", k=16)
                ln_to(XC[:, 1024 * (c % 6):1024 * (c % 6) + 1024], XN[:, 0:1024],
                      extra=(xm_tok[c],))
                xnt_tok[0] = transposes(lambda j: XN[:, 128 * j:128 * (j + 1)],
                           XNT[:, :].rearrange("p (j t) -> p j t", t=512)
                           [:, :, 128 * c4:128 * c4 + 128], 8)
            for ct in range(16):
                bank = G[ct % 4]
                for j in range(8):
                    w = ()
                    if j == 0:
                        w = (fe_tok[ct % 4],) if fe_tok[ct % 4] else (dve, act)
                        if ct == 0:
                            w = w + (xnt_tok[0],)
                    p.emit("tensor", lambda e, j=j, b=bank, ct=ct: e.matmul(
                        b[:, :], Wq[:, 3072 * j + 128 * ct:3072 * j + 128 * ct + 128],
                        XNT[:, 512 * j:512 * (j + 1)], start=(j == 0), stop=(j == 7)),
                        waits=w, inc=pe)
                dst = (QT if ct < 8 else KT)[:, 512 * (ct % 8):512 * (ct % 8) + 512]
                fe_tok[ct % 4] = feature_evac(bank, dst,
                    bias_col=BqkC[:, ct:ct + 1], nbias_col=NBqkC[:, ct:ct + 1])
            fe_last[0] = fe_tok[3]

        ao_ev = [None, None]

        def ln2_epilogue(c):
            p.emit("vector", lambda e: e.reduce_sum(out=mcol, in_=AO[:, :],
                   axis=AX.X), waits=(ao_ev[1],), inc=dve)
            p.emit("vector", lambda e: e.tensor_scalar_mul(mcol, mcol, 1.0 / D),
                   inc=dve)
            p.emit("vector", lambda e: e.tensor_scalar(
                out=F1[:, :], in0=AO[:, :], scalar1=mcol,
                scalar2=None, op0=OP.subtract),
                waits=(p1_sq[0],) if p1_sq[0] else (), inc=dve)
            p1_sq[0] = p.emit("scalar", lambda e: e.activation(out=F2[:, :],
                   in_=F1[:, :], func=AF.Square, accum_out=vcol),
                   waits=(dve,), inc=act)
            p.emit("vector", lambda e: e.tensor_scalar(out=vcol, in0=vcol,
                   scalar1=1.0 / D, scalar2=LN_EPS, op0=OP.mult, op1=OP.add),
                   waits=(act,), inc=dve)
            p.emit("scalar", lambda e: e.activation(out=vc2, in_=vcol, func=AF.Ln),
                   waits=(dve,), inc=act)
            p.emit("scalar", lambda e: e.activation(out=vc2, in_=vc2, func=AF.Exp,
                   scale=-0.5), inc=act)
            p.emit("vector", lambda e: e.scalar_tensor_tensor(out=F2[:, :],
                   in0=F1[:, :], scalar=vc2, in1=Gam[:, :], op0=OP.mult, op1=OP.mult),
                   waits=(act,), inc=dve)
            p.emit("vector", lambda e: e.tensor_tensor(out=F2[:, :], in0=F2[:, :],
                   in1=Bet[:, :], op=OP.add), inc=dve)
            xc = XC[:, 1024 * (c % 6):1024 * (c % 6) + 1024]
            p.emit("vector", lambda e, xc=xc: e.tensor_tensor(out=xc, in0=xc,
                   in1=F2[:, :], op=OP.add), inc=dve)
            p.emit("gpsimd", lambda e, c=c, xc=xc: e.dma_start(
                out=d_y[128 * c:128 * (c + 1), :], in_=xc), waits=(dve,),
                inc="out", k=16)

        def ffn_quant(c):
            xq_src = XC[:, 1024 * (c % 6):1024 * (c % 6) + 1024]
            p.emit("vector", lambda e, s=xq_src: e.reduce_sum(out=mcol, in_=s,
                   axis=AX.X), inc=dve)
            p.emit("vector", lambda e: e.tensor_scalar_mul(mcol, mcol, 1.0 / D),
                   inc=dve)
            p.emit("vector", lambda e, s=xq_src: e.tensor_scalar(out=F1[:, :],
                   in0=s, scalar1=mcol, scalar2=None, op0=OP.subtract),
                   waits=(p1_sq[0],) if p1_sq[0] else (), inc=dve)
            p1_sq[0] = p.emit("scalar", lambda e: e.activation(out=F2[:, :],
                   in_=F1[:, :], func=AF.Square, accum_out=vcol),
                   waits=(dve,), inc=act)
            p.emit("vector", lambda e: e.tensor_scalar(out=vcol, in0=vcol,
                   scalar1=1.0 / D, scalar2=LN_EPS, op0=OP.mult, op1=OP.add),
                   waits=(act,), inc=dve)
            p.emit("scalar", lambda e: e.activation(out=vc2, in_=vcol, func=AF.Ln),
                   waits=(dve,), inc=act)
            p.emit("scalar", lambda e: e.activation(out=vc2, in_=vc2, func=AF.Exp,
                   scale=-0.5), inc=act)
            t_f2 = p.emit("vector", lambda e: e.tensor_scalar(out=XQN[:, :],
                   in0=F1[:, :], scalar1=vc2, scalar2=None, op0=OP.mult),
                   waits=(act, "out"), inc=dve)
            p.emit("gpsimd", lambda e, c=c: e.dma_start(
                out=d_xs[128 * c:128 * (c + 1), :], in_=XQN[:, :]),
                waits=(t_f2,), inc="out", k=16)

        att_tok = [None]
        p2_sst = [None]

        def attn_chunk(g, c4, upto="zz"):
            c = 4 * g + c4
            v_ev = [None, None]
            stm_toks = []
            for n in range(2):
                bank = G[n]
                for j in range(8):
                    w = ()
                    if j == 0:
                        if ao_ev[0]:
                            w = (ao_ev[0], ao_ev[1])
                        else:
                            w = (dve, act)
                        if c4 == 0:
                            w = w + (fe_tok[0], fe_tok[1], xnt_tok[0])
                    p.emit("tensor", lambda e, j=j, b=bank, n=n, c4=c4: e.matmul(
                        b[:, :], XNT[:, 512 * j + 128 * c4:512 * j + 128 * c4 + 128],
                        Wq[:, 3072 * j + 2048 + 512 * n:3072 * j + 2048 + 512 * n + 512],
                        start=(j == 0), stop=False),
                        waits=w, inc=pe)
                p.emit("tensor", lambda e, b=bank, n=n: e.matmul(
                    b[:, :], OnesF[0:1, 0:128], BqvR[0:1, 512 * n:512 * n + 512],
                    start=False, stop=True), inc=pe)
                dst = Vp[:, 520 * n:520 * n + 520].rearrange(
                    "p (h f) -> p h f", f=65)[:, :, 0:64]
                v_ev[n] = p.emit("vector", lambda e, b=bank, d=dst: e.tensor_copy(
                    out=d, in_=b[:, :].rearrange("p (h f) -> p h f", f=64)),
                    waits=(pe,), inc=dve)
            if upto == "v": return
            tok_ktm = transposes(lambda j, c4=c4: KT[:, 512 * j + 128 * c4:512 * j + 128 * c4 + 128],
                       KTM[:, :].rearrange("p (j t) -> p j t", t=128), 8,
                       src_tok=fe_last[0])
            if upto == "ktm": return
            kvs_tok = p.emit("vector", lambda e: e.tensor_copy(out=KVs[:, :],
                   in_=SST[:, :]), inc=dve)
            if upto == "kvs": return
            for grp in range(2):
                first = True
                for half in range(2):
                    bank = G[half]
                    for a in range(4):
                        h = 8 * half + 2 * a + grp
                        bp = 64 * grp
                        kap = KT[bp:bp + 64,
                                 512 * (h // 2) + 128 * c4:512 * (h // 2) + 128 * c4 + 128]
                        qap = QT[bp:bp + 64,
                                 512 * (h // 2) + 128 * c4:512 * (h // 2) + 128 * c4 + 128]
                        w = ()
                        if first:
                            w = (pe, v_ev[0], v_ev[1]) + tuple(stm_toks)
                        p.emit("tensor", lambda e, b=bank, a=a, kap=kap, qap=qap, bp=bp:
                               e.matmul(b[:, 128 * a:128 * a + 128], kap, qap,
                                        start=(a == 0), stop=(a == 3),
                                        tile_position=(bp, 0)),
                               waits=w, inc=pe)
                        first = False
                if upto != "stmm":
                    for half in range(2):
                        dst = STm[:, 1024 * half:1024 * half + 1024].rearrange(
                            "p (a r) -> p a r", r=256)[:, :, 128 * grp:128 * grp + 128]
                        src3 = G[half][:, :].rearrange("p (a t) -> p a t", t=128)
                        mt3 = Mtri[:, :].rearrange("p (a t) -> p a t", t=128)
                        stm_toks.append(p.emit("vector", lambda e, d=dst, s=src3,
                            m=mt3: e.tensor_tensor(
                            out=d, in0=s, in1=m, op=OP.mult), waits=(pe,), inc=dve))
            if upto in ("st", "stmm"): return
            if c >= 1:
                ln2_epilogue(c - 1)
                ffn_quant(c - 1)
            def nbank(h):
                b = G[2] if h < 7 else (G[3] if h < 14 else G[4])
                return b, 65 * (h - (0 if h < 7 else (7 if h < 14 else 14)))
            for h in range(H):
                b, c0 = nbank(h)
                p.emit("tensor", lambda e, b=b, c0=c0, h=h: e.matmul(
                    b[:, c0:c0 + 65], STm[:, 128 * h:128 * h + 128],
                    Vp[:, 65 * h:65 * h + 65], start=(h in (0, 7, 14)), stop=False),
                    waits=(pe,) + tuple(stm_toks) if h == 0 else (), inc=pe)
            for grp in range(2):
                first = True
                for idx in range(8):
                    h = 2 * idx + grp
                    b, c0 = nbank(h)
                    bp = 64 * grp
                    qap = QT[bp:bp + 64,
                             512 * (h // 2) + 128 * c4:512 * (h // 2) + 128 * c4 + 128]
                    p.emit("tensor", lambda e, b=b, c0=c0, qap=qap, h=h, bp=bp:
                           e.matmul(b[:, c0:c0 + 65], qap, kvs_slice(h),
                                    start=False, stop=(h in (5, 13, 15)),
                                    tile_position=(bp, 0)),
                           waits=(pe, kvs_tok) if first else (), inc=pe)
                    first = False
            if upto == "num": return
            for (bi, h0, k) in ((2, 0, 7), (3, 7, 7), (4, 14, 2)):
                bank = G[bi]
                dcv = DC[:, h0:h0 + k]
                b3 = bank[:, 0:65 * k].rearrange("p (h f) -> p h f", f=65)
                p.emit("vector", lambda e, b3=b3, d=dcv, k=k: e.tensor_scalar(
                    out=d, in0=b3[:, :, 64], scalar1=DEN_EPS, scalar2=None, op0=OP.add),
                    waits=(pe,), inc=dve)
                p.emit("vector", lambda e, d=dcv: e.reciprocal(out=d, in_=d), inc=dve)
                att3 = ATT[:, 64 * h0:64 * (h0 + k)].rearrange("p (h f) -> p h f", f=64)
                att_tok[0] = p.emit("vector", lambda e, b3=b3, a3=att3, d=dcv, k=k:
                    e.tensor_tensor(
                    out=a3, in0=b3[:, :, 0:64],
                    in1=d.unsqueeze(2).broadcast_to((128, k, 64)), op=OP.mult), inc=dve)
            if upto == "div": return
            for grp in range(2):
                first = True
                for idx in range(8):
                    h = 2 * idx + grp
                    p.emit("tensor", lambda e, h=h, idx=idx: e.matmul(
                        state_slice(h), KTM[:, 64 * h:64 * h + 64],
                        Vp[:, 65 * h:65 * h + 65], start=(idx in (0, 4)),
                        stop=(idx in (3, 7)),
                        tile_position=(0, 64 * (h % 2))),
                        waits=((pe, tok_ktm) + ((p2_sst[0],) if p2_sst[0] else ()))
                        if first else (), inc=pe)
                    first = False
            p.emit("vector", lambda e: e.tensor_tensor(out=SST[:, 0:260],
                   in0=SST[:, 0:260], in1=S0[:, 0:260], op=OP.add),
                   waits=(pe,), inc=dve)
            p2_sst[0] = p.emit("vector", lambda e: e.tensor_tensor(
                   out=SST[:, 260:520],
                   in0=SST[:, 260:520], in1=S1[:, 0:260], op=OP.add), inc=dve)
            if upto == "state": return
            att_ev = transposes(lambda j: ATT[:, 128 * j:128 * (j + 1)],
                       ATT[:, :].rearrange("p (j t) -> p j t", t=128), 8,
                       src_tok=att_tok[0])
            if upto == "att": return
            for n in range(2):
                bank = G[n]
                for j in range(8):
                    p.emit("tensor", lambda e, j=j, b=bank, n=n: e.matmul(
                        b[:, :], ATT[:, 128 * j:128 * j + 128],
                        Wo[:, 1024 * j + 512 * n:1024 * j + 512 * n + 512],
                        start=(j == 0), stop=False),
                        waits=(att_ev,) if j == 0 else (), inc=pe)
                p.emit("tensor", lambda e, b=bank, n=n: e.matmul(
                    b[:, :], OnesF[0:1, 0:128], BoR[0:1, 512 * n:512 * n + 512],
                    start=False, stop=True), inc=pe)
            if upto == "proj": return
            for n in range(2):
                ao_ev[n] = p.emit("vector", lambda e, n=n: e.tensor_copy(
                    out=AO[:, 512 * n:512 * n + 512], in_=G[n][:, :]),
                    waits=(pe,) if n == 0 else (), inc=dve)

        if STOP_AFTER in ("p2qkv", "p2c1"):
            qkv_stage(0)
            if STOP_AFTER == "p2c1":
                attn_chunk(0, 0, upto=UPTO)
                for (i, ap) in enumerate((XN[:, :], QT[:, 0:1024], KT[:, 0:1024],
                                          KTM[:, :], Vp[:, 0:1024], STm[:, 0:1024],
                                          ATT[:, :], KVs[:, 0:520])):
                    p.emit("gpsimd", lambda e, i=i, ap=ap: e.dma_start(
                        out=d_y[128 * i:128 * i + 128, 0:ap.shape[1]], in_=ap),
                        waits=(dve, act, pe), inc="out", k=16)
        else:
            for g in range(4):
                qkv_stage(g)
                for c4 in range(4):
                    attn_chunk(g, c4)
            ln2_epilogue(NCH - 1)
            ffn_quant(NCH - 1)

        p.join()
        if STOP_AFTER in ("p2", "p2qkv", "p2c1"):
            p.emit("gpsimd", lambda e: e.engine_nop(), waits=("out",))
            p.flush(block)
            s12.close()
            return nc
        P2END = (p.tok(pe), p.tok(dve), p.tok(act), p.tok("pool"))
        p.flush(block)
        s2q.close()
        s12.close()

        s3 = ExitStack()
        sb3 = lambda n, shp, dt: s3.enter_context(nc.sbuf_tensor(n, shp, dt))
        XQT = sb3("XQT", [128, 8 * 512], BF16)
        WG = sb3("WG", [128, 22 * 1024], BF16)
        WU = sb3("WU", [128, 22 * 1024], BF16)
        WD = sb3("WD", [128, 16 * 1408], BF16)
        SBB = sb3("SBB", [128, 2 * 176], BF16)
        Q_XB = sb3("qXB", [128, 2 * 1024], F32)
        Q_XQ = sb3("qXQ", [128, 1024], BF16)
        FAx = sb3("FAx", [128, 2816], BF16)
        B1x = sb3("B1x", [128, 2816], BF16)
        B2x = sb3("B2x", [128, 2816], BF16)
        MBx = sb3("MBx", [128, 176], F32)
        SBx = sb3("SBx", [128, 176], F32)
        T1x = sb3("T1x", [128, 176], F32)
        T2x = sb3("T2x", [128, 176], F32)
        F_H = sb3("FH", [128, 2816], F32)
        HQ = sb3("HQ", [128, 2816], BF16)
        HQT = sb3("HQT", [128, 22 * 128], BF16)
        F2s = sb3("F2s", [128, 1024], BF16)
        SUb = sb3("SUb", [128, 1024], BF16)
        EV = sb3("EV", [128, 1024], F32)

        chains = ["lds", "wl2", "ldx", "wl3", "wl4", "wl5", "wl6", "wl7",
                  "wl8", "wl9", "wlA", "wlB", "wlC", "wlD", "wlE", "wlF"]
        ch_tok = [None] * 16
        for i in range(60):
            if i < 22:
                dst = WG[:, 1024 * i:1024 * (i + 1)]
                w_ = 1024
                k4 = i % 12
            elif i < 44:
                dst = WU[:, 1024 * (i - 22):1024 * (i - 21)]
                w_ = 1024
                k4 = (i - 22) % 12
            else:
                dst = WD[:, 1408 * (i - 44):1408 * (i - 43)]
                w_ = 1408
                k4 = 12 + (i - 44) % 4
            w = ("wout",) if ch_tok[k4] is None else (ch_tok[k4],)
            ch_tok[k4] = p.emit("sync", lambda e, d=dst, w_=w_, i=i: e.dma_start(
                out=d[:, 0:w_], in_=d_wq[i, :, 0:w_]), waits=w,
                inc=chains[k4], k=16)
        TOK_WLOAD = tuple(ch_tok[0:12])
        TOK_WDLOAD = tuple(ch_tok[12:16])

        prev_xqtp = [None]
        htp_tok = [None]
        dn_toks = [None, None]
        ev_toks = [None, None]
        hq_toks = [None] * NCH
        last_q = [None]
        xqe_tok = [None] * NCH
        xq_tok = [None] * (NCH + 1)
        xq_tok[0] = p.emit("sync", lambda e: e.dma_start(
            out=Q_XB[:, 0:1024], in_=d_xs[0:128, :]), waits=("out",),
            inc="ldp0", k=16)

        def xload(c):
            if c >= NCH:
                return
            nbuf = Q_XB[:, 1024 * (c % 2):1024 * (c % 2) + 1024]
            s_ = d_xs[128 * c:128 * (c + 1), :]
            w = (xq_tok[c - 1],)
            if prev_xqtp[0]:
                w = w + (prev_xqtp[0],)
            xq_tok[c] = p.emit("sync", lambda e, a=nbuf, b=s_: e.dma_start(
                out=a, in_=b), waits=w,
                inc="ldp0" if c % 2 == 0 else "ldp1", k=16)

        xq_dst = [None]

        def xquant_dve(c):
            w0 = (xq_tok[c],)
            if last_q[0]:
                w0 = w0 + (last_q[0],)
            if c == 0:
                w0 = w0 + (P2END[1], P2END[2])
            dw = (prev_xqtp[0],) if prev_xqtp[0] else (P2END[0],)
            xq_dst[0] = quant(Q_XB[:, 1024 * (c % 2):1024 * (c % 2) + 1024],
                              1024, Q_XQ[:, :], FAx, B1x, B2x,
                              MBx, SBx, T1x, T2x, SBB, w0=w0, dst_w=dw)[0]
            last_q[0] = xq_dst[0]

        def xq_tposes(c):
            xqe_tok[c] = transposes(
                lambda j: Q_XQ[:, 128 * j:128 * (j + 1)],
                XQT[:, :].rearrange("p (j t) -> p j t", t=512)
                [:, :, 128 * (c % 4):128 * (c % 4) + 128], 8,
                src_tok=xq_dst[0])
            prev_xqtp[0] = p.tok(pe)

        ITILES = [(0, 512), (512, 512), (1024, 512), (1536, 512), (2048, 512),
                  (2560, 256)]
        tok_actfree = [None, None]
        tok_fh_par = [None, None]
        tok_fh2 = [None]
        tok_poolFH = [None]
        WG3 = WG[:, :].rearrange("p (t j r) -> p t j r", j=8, r=128)
        WU3 = WU[:, :].rearrange("p (t j r) -> p t j r", j=8, r=128)
        WD3 = WD[:, :].rearrange("p (k c) -> p k c", c=2816)

        def gates_silu(c):
            for ti, (i0_, ilen) in enumerate(ITILES):
                it0 = i0_ // 128
                nt = ilen // 128
                par = ti % 2
                Gg, Gu = G[2 * par], G[2 * par + 1]
                for (bank, Wt3) in ((Gg, WG3), (Gu, WU3)):
                    for j in range(8):
                        w = ()
                        if j == 0 and bank is Gg:
                            if tok_actfree[par]:
                                w = (tok_actfree[par],)
                            else:
                                w = P2END[:3] + TOK_WLOAD
                            if ti == 0:
                                w = w + (xqe_tok[c],)
                        p.emit("tensor", lambda e, j=j, b=bank, W3=Wt3, it0=it0,
                               nt=nt, il=ilen, c=c: e.matmul(
                               b[:, 0:il],
                               XQT[:, 512 * j + 128 * (c % 4):512 * j + 128 * (c % 4) + 128],
                               W3[:, it0:it0 + nt, j, :],
                               start=(j == 0), stop=(j == 7)),
                               waits=w, inc=pe)
                tok_pe_gates = p.tok(pe)
                w_sil = (tok_pe_gates,)
                w_sil += (tok_fh_par[par],) if tok_fh_par[par] else (P2END[3],)
                p.emit("scalar", lambda e, il=ilen, Gg=Gg, par=par:
                    e.activation(out=F2s[:, 512 * par:512 * par + il],
                    in_=Gg[:, 0:il], func=AF.Silu),
                    waits=w_sil, inc=act)
                tok_su = p.emit("scalar", lambda e, il=ilen, Gu=Gu, par=par: e.copy(
                    out=SUb[:, 512 * par:512 * par + il], in_=Gu[:, 0:il]), inc=act)
                w_fh = ()
                if i0_ == 0:
                    w_fh = hq_toks[c - 1] if c >= 1 else (P2END[1], P2END[2])
                tk = p.emit("gpsimd", lambda e, i0_=i0_, il=ilen, par=par:
                    e.tensor_tensor(
                    out=F_H[:, i0_:i0_ + il], in0=F2s[:, 512 * par:512 * par + il],
                    in1=SUb[:, 512 * par:512 * par + il],
                    op=OP.mult), waits=(tok_su,) + w_fh, inc="pool")
                tok_actfree[par] = tok_su
                tok_fh_par[par] = tk
                if ti == 2:
                    tok_fh2[0] = tk
                if i0_ == ITILES[-1][0]:
                    tok_poolFH[0] = tk

        def tposes_down(c):
            for half in range(2):
                done = 0
                while done < 11:
                    nb_t = min(8, 11 - done)
                    for j in range(nb_t):
                        w = ()
                        if j == 0:
                            if tp_free[0]:
                                w = (tp_free[0],)
                            if done == 0 and half == 0:
                                w = w + hq_toks[c]
                        p.emit("tensor", lambda e, j=j, d0=done, half=half: e.transpose(
                            TP[:, 128 * j:128 * (j + 1)],
                            HQ[:, 1408 * half + 128 * (d0 + j):
                                1408 * half + 128 * (d0 + j + 1)], Id[:, :]),
                            waits=w, inc=pe)
                    tp_free[0] = p.emit("scalar", lambda e, d0=done, n=nb_t, half=half:
                        e.copy(
                        out=HQT[:, :].rearrange("p (j t) -> p j t", t=128)
                        [:, 11 * half + d0:11 * half + d0 + n, :],
                        in_=TP[:, 0:128 * n].rearrange("p (j t) -> p j t", t=128)),
                        waits=(pe,), inc=act)
                    done += nb_t
            htp_tok[0] = p.tok(pe)
            for n in range(2):
                bank = (S0, S1)[n]
                for it in range(22):
                    half = it // 11
                    jj = it % 11
                    off = 1408 * half + 128 * jj
                    rhs = WD3[:, 4 * n:4 * n + 4, off:off + 128]
                    w = ()
                    if n == 0 and it == 0:
                        w = (tp_free[0],)
                        if ev_toks[0]:
                            w = w + (ev_toks[0], ev_toks[1])
                        else:
                            w = w + (P2END[1], P2END[2]) + TOK_WDLOAD
                    p.emit("tensor", lambda e, it=it, b=bank, rhs=rhs: e.matmul(
                        b[:, :], HQT[:, 128 * it:128 * (it + 1)], rhs,
                        start=(it == 0), stop=(it == 21)),
                        waits=w, inc=pe)
                dn_toks[n] = p.tok(pe)

        def hquant(c):
            w0 = (tok_fh2[0],)
            if last_q[0]:
                w0 = w0 + (last_q[0],)
            if c == 0:
                w0 = w0 + (P2END[1], P2END[2])
            dw = (htp_tok[0],) if htp_tok[0] else (P2END[0],)
            src3a = F_H[:, 0:1408].rearrange("p (b s) -> p b s", s=16)
            src3b = F_H[:, 1408:2816].rearrange("p (b s) -> p b s", s=16)
            p.emit("vector", lambda e: e.tensor_reduce(out=MBx[:, 0:88], in_=src3a,
                   axis=AX.X, op=OP.max, apply_absolute_value=True),
                   waits=w0, inc=dve)
            p.emit("vector", lambda e: e.tensor_scalar(
                   out=FAx[:, 0:1408], in0=F_H[:, 0:1408], scalar1=0.0,
                   scalar2=None, op0=OP.max), inc=dve)
            p.emit("vector", lambda e: e.tensor_scalar(
                   out=B1x[:, 0:1408], in0=F_H[:, 0:1408], scalar1=-1.0,
                   scalar2=0.0, op0=OP.mult, op1=OP.max), inc=dve)
            tok_abs0 = p.emit("vector", lambda e: e.tensor_tensor(
                   out=FAx[:, 0:1408],
                   in0=FAx[:, 0:1408], in1=B1x[:, 0:1408], op=OP.add), inc=dve)
            p.emit("vector", lambda e: e.tensor_reduce(out=MBx[:, 88:176],
                   in_=src3b,
                   axis=AX.X, op=OP.max, apply_absolute_value=True),
                   waits=(tok_poolFH[0],), inc=dve)
            p.emit("vector", lambda e: e.tensor_scalar_mul(T1x[:, 0:176],
                   MBx[:, 0:176], 0.5), inc=dve)
            p.emit("vector", lambda e: e.tensor_scalar_mul(T2x[:, 0:176],
                   T1x[:, 0:176], C3), inc=dve)
            p.emit("vector", lambda e: e.scalar_tensor_tensor(out=MBx[:, 0:176],
                   in0=T2x[:, 0:176], scalar=-2.0, in1=T1x[:, 0:176], op0=OP.mult,
                   op1=OP.add), inc=dve)
            p.emit("vector", lambda e: e.scalar_tensor_tensor(out=T1x[:, 0:176],
                   in0=T2x[:, 0:176], scalar=-1.0, in1=MBx[:, 0:176], op0=OP.mult,
                   op1=OP.add), inc=dve)
            p.emit("vector", lambda e: e.scalar_tensor_tensor(out=SBx[:, 0:176],
                   in0=T1x[:, 0:176], scalar=C3, in1=T2x[:, 0:176], op0=OP.mult,
                   op1=OP.add), inc=dve)
            p.emit("vector", lambda e: e.tensor_scalar_max(SBx[:, 0:176],
                   SBx[:, 0:176], 1e-12), inc=dve)
            p.emit("vector", lambda e: e.reciprocal(out=T2x[:, 0:176],
                   in_=SBx[:, 0:176]), inc=dve)
            p.emit("vector", lambda e: e.tensor_copy(out=SBB[:, 0:176],
                   in_=T2x[:, 0:176]), inc=dve)
            p.emit("vector", lambda e: e.tensor_copy(out=SBB[:, 176:352],
                   in_=SBx[:, 0:176]), inc=dve)
            FA3 = FAx[:, 0:2816].rearrange("p (b s) -> p b s", s=16)
            p.emit("vector", lambda e: e.tensor_scalar(out=FAx[:, 1408:2816],
                   in0=F_H[:, 1408:2816], scalar1=0.0, scalar2=None, op0=OP.max),
                   inc=dve)
            p.emit("vector", lambda e: e.tensor_scalar(out=B1x[:, 1408:2816],
                   in0=F_H[:, 1408:2816], scalar1=-1.0, scalar2=0.0, op0=OP.mult,
                   op1=OP.max), inc=dve)
            p.emit("vector", lambda e: e.tensor_tensor(out=FAx[:, 1408:2816],
                   in0=FAx[:, 1408:2816], in1=B1x[:, 1408:2816], op=OP.add),
                   inc=dve)
            rb = SBB[:, 0:176].unsqueeze(2).broadcast_to((128, 176, 16))
            p.emit("vector", lambda e, rb=rb: e.tensor_tensor(out=FA3, in0=FA3,
                   in1=rb, op=OP.mult), inc=dve)

            def hcmp(i, dst):
                return lambda e: e.tensor_single_scalar(out=dst,
                       in_=FAx[:, 0:2816], scalar=THR[i], op=OP.is_gt)

            p.emit("vector", hcmp(0, B1x[:, 0:2816]), inc=dve)
            for i in (1, 2, 3):
                p.emit("vector", hcmp(i, B2x[:, 0:2816]), inc=dve)
                p.emit("vector", lambda e: e.tensor_tensor(out=B1x[:, 0:2816],
                       in0=B1x[:, 0:2816], in1=B2x[:, 0:2816], op=OP.add),
                       inc=dve)
            p.emit("vector", lambda e: e.tensor_scalar_mul(B1x[:, 0:2816],
                   B1x[:, 0:2816], 0.5), inc=dve)
            for i in (4, 5):
                p.emit("vector", hcmp(i, B2x[:, 0:2816]), inc=dve)
                p.emit("vector", lambda e: e.tensor_tensor(out=B1x[:, 0:2816],
                       in0=B1x[:, 0:2816], in1=B2x[:, 0:2816], op=OP.add),
                       inc=dve)
            p.emit("vector", hcmp(6, B2x[:, 0:2816]), inc=dve)
            p.emit("vector", lambda e: e.scalar_tensor_tensor(out=B1x[:, 0:2816],
                   in0=B2x[:, 0:2816], scalar=2.0, in1=B1x[:, 0:2816],
                   op0=OP.mult, op1=OP.add), inc=dve)
            p.emit("vector", lambda e: e.tensor_single_scalar(out=FAx[:, 0:2816],
                   in_=F_H[:, 0:2816], scalar=0.0, op=OP.is_gt), inc=dve)
            p.emit("vector", lambda e: e.tensor_scalar(out=FAx[:, 0:2816],
                   in0=FAx[:, 0:2816], scalar1=2.0, scalar2=-1.0, op0=OP.mult,
                   op1=OP.add), inc=dve)
            sb_b = SBB[:, 176:352].unsqueeze(2).broadcast_to((128, 176, 16))
            p.emit("vector", lambda e, sb_b=sb_b: e.tensor_tensor(
                   out=B2x[:, 0:2816].rearrange("p (b s) -> p b s", s=16),
                   in0=B1x[:, 0:2816].rearrange("p (b s) -> p b s", s=16),
                   in1=sb_b, op=OP.mult), inc=dve)
            dw = (htp_tok[0],) if htp_tok[0] else (P2END[0],)
            tok_d = p.emit("vector", lambda e: e.tensor_tensor(
                   out=HQ[:, 0:2816], in0=B2x[:, 0:2816],
                   in1=FAx[:, 0:2816], op=OP.mult), waits=dw, inc=dve)
            last_q[0] = tok_d
            hq_toks[c] = (tok_d,)

        def evac_ship(c):
            for n in range(2):
                w = (dn_toks[n], "out") if n == 0 else (dn_toks[n],)
                if c == 0:
                    w = w + P2END
                ev_toks[n] = p.emit("scalar", lambda e, n=n: e.copy(
                    out=EV[:, 512 * n:512 * (n + 1)], in_=(S0, S1)[n][:, :]),
                    waits=w, inc=act)
            p.emit("gpsimd", lambda e, c=c: e.dma_start(
                out=d_y[128 * c:128 * (c + 1), :], in_=EV[:, :],
                accum_op=OP.add), waits=(ev_toks[0], ev_toks[1]), inc="out", k=16)

        xquant_dve(0)
        xq_tposes(0)
        for c in range(NCH):
            xload(c + 1)
            gates_silu(c)
            if c + 1 < NCH:
                xquant_dve(c + 1)
                xq_tposes(c + 1)
            if c >= 1:
                tposes_down(c - 1)
            hquant(c)
            if c >= 1:
                evac_ship(c - 1)
        tposes_down(NCH - 1)
        evac_ship(NCH - 1)

        p.emit("gpsimd", lambda e: e.engine_nop(), waits=("out",))
        p.flush(block)
        s3.close()

    return nc


def kernel(L0_attn_x, L0_attn_Wqkv, L0_attn_bqkv, L0_attn_Wout, L0_attn_bout,
           L0_attn_gamma, L0_attn_beta, L0_ffn_w_gate, L0_ffn_w_up, L0_ffn_w_down):
    f32 = np.float32
    x = np.ascontiguousarray(np.asarray(L0_attn_x, f32).reshape(8 * T, D))
    wqkvT = np.ascontiguousarray(np.asarray(L0_attn_Wqkv, f32).T)
    bqkv = np.asarray(L0_attn_bqkv, f32)
    bqk_cols = np.ascontiguousarray(bqkv[:2048].reshape(16, 128).T)
    bqkk_row = np.ascontiguousarray(bqkv[1024:2048].reshape(1, D))
    bqv_row = np.ascontiguousarray(bqkv[2048:].reshape(1, D))
    woutT = np.ascontiguousarray(np.asarray(L0_attn_Wout, f32).T)
    bo_row = np.ascontiguousarray(np.asarray(L0_attn_bout, f32).reshape(1, D))
    gamma_t = np.ascontiguousarray(np.broadcast_to(
        np.asarray(L0_attn_gamma, f32).reshape(1, D), (128, D)))
    beta_t = np.ascontiguousarray(np.broadcast_to(
        np.asarray(L0_attn_beta, f32).reshape(1, D), (128, D)))
    wg = np.ascontiguousarray(np.asarray(L0_ffn_w_gate, f32))
    wu = np.ascontiguousarray(np.asarray(L0_ffn_w_up, f32))
    wd = np.ascontiguousarray(np.asarray(L0_ffn_w_down, f32))
    ident = np.eye(128, dtype=ml_dtypes.bfloat16)
    tri = np.triu(np.ones((128, 128), np.float32))
    mtri = np.ascontiguousarray(np.tile(tri, (1, 4))).astype(ml_dtypes.bfloat16)
    onesf = np.ones((1, 128), f32)
    zeros = np.zeros((T, D), f32)

    shared = dict(wqkvT=wqkvT, bqk_cols=bqk_cols, bqkk_row=bqkk_row,
                  bqv_row=bqv_row, woutT=woutT, bo_row=bo_row,
                  gamma_t=gamma_t, beta_t=beta_t, wg=wg, wu=wu, wd=wd,
                  ident=ident, mtri=mtri, onesf=onesf)
    in_maps = []
    for c in range(8):
        d = dict(shared)
        d["xm"] = np.ascontiguousarray(x[c * T:(c + 1) * T])
        if c % 2 == 1:
            d["xp"] = np.ascontiguousarray(x[(c - 1) * T:c * T])
            d["mc"] = np.full((128, 1), 1.0, f32)
        else:
            d["xp"] = zeros
            d["mc"] = np.zeros((128, 1), f32)
        in_maps.append(d)

    nc = build_program()
    res = run_bass_kernel_spmd(nc, in_maps, core_ids=list(range(8)))
    out = np.concatenate([np.asarray(r["y"], np.float32) for r in res.results], axis=0)
    return out.reshape(4, 4096, D)


# version 64 (final)
# speedup vs baseline: 1.0386x; 1.0386x over previous
import numpy as np
import ml_dtypes
from contextlib import ExitStack

import concourse.bass as bass
import concourse.mybir as mybir
from concourse.bass_utils import run_bass_kernel_spmd

F32 = mybir.dt.float32
BF16 = mybir.dt.bfloat16
I16 = mybir.dt.int16
AX = mybir.AxisListType
OP = mybir.AluOpType
AF = mybir.ActivationFunctionType

D = 1024
T = 2048
NCH = 16
H = 16
DI = 2816
LN_EPS = 1e-5
FEPS = 1e-4
DEN_EPS = 8e-4
THR = [0.25, 0.75, 1.25, 1.75, 2.5, 3.5, 5.0]
C3 = 1.0 / 3.0

ENGINES = ("sync", "gpsimd", "tensor", "vector", "scalar")
SEMS = ("ldx", "ldw", "out", "pe", "dve", "act", "ldp0", "ldp1", "lds",
        "pool", "wout", "wl2", "wl3", "wl4", "wl5", "wl6", "wl7",
        "wl8", "wl9", "wlA", "wlB", "wlC", "wlD", "wlE", "wlF")
STOP_AFTER = None
UPTO = "zz"


class Prog:

    ENG_SEM = {"vector": "dve", "scalar": "act", "gpsimd": "pool"}

    def __init__(self):
        self.ops = {e: [] for e in ENGINES}
        self.n = {s: 0 for s in SEMS}
        self.sems = {}
        self.last_wait = {e: {} for e in ENGINES}

    def tok(self, s):
        return (s, self.n[s])

    def emit(self, eng, fn, waits=(), inc=None, k=1):
        wl = tuple((w, self.n[w]) if isinstance(w, str) else w for w in waits)
        es = self.ENG_SEM.get(eng)
        if es is not None:
            wl = wl + ((es, self.n[es]),)
        tok = None
        if inc is not None:
            self.n[inc] += k
            tok = (inc, self.n[inc])
        self.ops[eng].append((wl, fn, inc, k))
        return tok

    def join(self):
        toks = tuple((s, self.n[s]) for s in SEMS)
        for e in ENGINES:
            self.ops[e].append((toks, None, None, 0))

    def flush(self, block):
        for e in ENGINES:
            lst = self.ops[e]
            self.ops[e] = []
            if not lst:
                continue

            def section(eng, lst=lst, e=e):
                lw = self.last_wait[e]
                for (wl, fn, inc, k) in lst:
                    for (s, c) in wl:
                        if c > 0 and lw.get(s, 0) < c:
                            eng.wait_ge(self.sems[s], c)
                            lw[s] = c
                    if fn is not None:
                        inst = fn(eng)
                        if inc is not None:
                            inst.then_inc(self.sems[inc], k)

            getattr(block, e)(section)


def build_program():
    nc = bass.Bass("TRN2", target_bir_lowering=False)

    d_xm = nc.dram_tensor("xm", [T, D], F32, kind="ExternalInput").ap()
    d_xp = nc.dram_tensor("xp", [T, D], F32, kind="ExternalInput").ap()
    d_mc = nc.dram_tensor("mc", [128, 1], F32, kind="ExternalInput").ap()
    d_wqkvT = nc.dram_tensor("wqkvT", [D, 3 * D], F32, kind="ExternalInput").ap()
    d_bqk_cols = nc.dram_tensor("bqk_cols", [128, 16], F32, kind="ExternalInput").ap()
    d_bqkk_row = nc.dram_tensor("bqkk_row", [1, D], F32, kind="ExternalInput").ap()
    d_bqv_row = nc.dram_tensor("bqv_row", [1, D], F32, kind="ExternalInput").ap()
    d_woutT = nc.dram_tensor("woutT", [D, D], F32, kind="ExternalInput").ap()
    d_bo_row = nc.dram_tensor("bo_row", [1, D], F32, kind="ExternalInput").ap()
    d_gamma = nc.dram_tensor("gamma_t", [128, D], F32, kind="ExternalInput").ap()
    d_beta = nc.dram_tensor("beta_t", [128, D], F32, kind="ExternalInput").ap()
    d_wg = nc.dram_tensor("wg", [DI, D], F32, kind="ExternalInput").ap()
    d_wu = nc.dram_tensor("wu", [DI, D], F32, kind="ExternalInput").ap()
    d_wd = nc.dram_tensor("wd", [D, DI], F32, kind="ExternalInput").ap()
    d_id = nc.dram_tensor("ident", [128, 128], BF16, kind="ExternalInput").ap()
    d_mtri = nc.dram_tensor("mtri", [128, 512], BF16, kind="ExternalInput").ap()
    d_ones = nc.dram_tensor("onesf", [1, 128], F32, kind="ExternalInput").ap()
    d_y = nc.dram_tensor("y", [T, D], F32, kind="ExternalOutput").ap()
    d_wq = nc.dram_tensor("wqscr", [60, 128, 1408], BF16, kind="Internal").ap()
    d_xs = nc.dram_tensor("xscr", [T, D], F32, kind="Internal").ap()

    p = Prog()
    dve, act, pe = "dve", "act", "pe"

    with ExitStack() as root:
        for s in SEMS:
            p.sems[s] = root.enter_context(nc.semaphore(s))
        S0 = root.enter_context(nc.psum_tensor("S0", [128, 512], F32))
        S1 = root.enter_context(nc.psum_tensor("S1", [128, 512], F32))
        G = [root.enter_context(nc.psum_tensor(f"G{i}", [128, 512], F32))
             for i in range(5)]
        TP = root.enter_context(nc.psum_tensor("TP", [128, 1024], BF16))
        block = root.enter_context(nc.Block())

        Id = root.enter_context(nc.sbuf_tensor("Id", [128, 128], BF16))
        Mc = root.enter_context(nc.sbuf_tensor("Mc", [128, 1], F32))

        p.emit("sync", lambda e: e.dma_start(out=Id[:, :], in_=d_id), inc="ldx", k=16)
        p.emit("sync", lambda e: e.dma_start(out=Mc[:, :], in_=d_mc), inc="ldx", k=16)

        def quant(src_f32, F, dst_bf16, FA, B1, B2, MB, SBuf, T1, T2, SBB, w0=(),
                  dst_w=("pe",), xeng="vector", B3=None, aeng="vector"):
            nb = F // 16
            xinc = dve if xeng == "vector" else "pool"
            src3 = src_f32.rearrange("p (b s) -> p b s", s=16)
            p.emit("vector", lambda e: e.tensor_reduce(out=MB[:, 0:nb], in_=src3,
                   axis=AX.X, op=OP.max, apply_absolute_value=True),
                   waits=tuple(w0), inc=dve)
            p.emit("vector", lambda e: e.tensor_scalar_mul(T1[:, 0:nb], MB[:, 0:nb], 0.5),
                   inc=dve)
            p.emit("vector", lambda e: e.tensor_scalar_mul(T2[:, 0:nb], T1[:, 0:nb], C3),
                   inc=dve)
            p.emit("vector", lambda e: e.scalar_tensor_tensor(out=MB[:, 0:nb],
                   in0=T2[:, 0:nb], scalar=-2.0, in1=T1[:, 0:nb], op0=OP.mult,
                   op1=OP.add), inc=dve)
            p.emit("vector", lambda e: e.scalar_tensor_tensor(out=T1[:, 0:nb],
                   in0=T2[:, 0:nb], scalar=-1.0, in1=MB[:, 0:nb], op0=OP.mult,
                   op1=OP.add), inc=dve)
            p.emit("vector", lambda e: e.scalar_tensor_tensor(out=SBuf[:, 0:nb],
                   in0=T1[:, 0:nb], scalar=C3, in1=T2[:, 0:nb], op0=OP.mult,
                   op1=OP.add), inc=dve)
            p.emit("vector", lambda e: e.tensor_scalar_max(SBuf[:, 0:nb], SBuf[:, 0:nb],
                   1e-12), inc=dve)
            p.emit("vector", lambda e: e.reciprocal(out=T2[:, 0:nb], in_=SBuf[:, 0:nb]),
                   inc=dve)
            p.emit("vector", lambda e: e.tensor_copy(out=SBB[:, 0:nb], in_=T2[:, 0:nb]),
                   inc=dve)
            p.emit("vector", lambda e: e.tensor_copy(out=SBB[:, 176:176 + nb],
                   in_=SBuf[:, 0:nb]), inc=dve)
            if aeng == "scalar":
                tok_abs = p.emit("scalar", lambda e: e.activation(out=FA[:, 0:F],
                       in_=src_f32, func=AF.Abs), waits=tuple(w0), inc=act)
            else:
                p.emit("vector", lambda e: e.tensor_scalar(out=FA[:, 0:F],
                       in0=src_f32, scalar1=0.0, scalar2=None, op0=OP.max),
                       waits=tuple(w0), inc=dve)
                p.emit("vector", lambda e: e.tensor_scalar(out=B1[:, 0:F],
                       in0=src_f32,
                       scalar1=-1.0, scalar2=0.0, op0=OP.mult, op1=OP.max), inc=dve)
                tok_abs = p.emit("vector", lambda e: e.tensor_tensor(out=FA[:, 0:F],
                       in0=FA[:, 0:F], in1=B1[:, 0:F], op=OP.add), inc=dve)
            rb = SBB[:, 0:nb].unsqueeze(2).broadcast_to((128, nb, 16))
            tok_z = p.emit(xeng, lambda e: e.tensor_tensor(
                out=FA[:, 0:F].rearrange("p (b s) -> p b s", s=16),
                in0=FA[:, 0:F].rearrange("p (b s) -> p b s", s=16), in1=rb,
                op=OP.mult), waits=(tok_abs, dve), inc=xinc)

            def cmp(i, dst):
                return lambda e: e.tensor_single_scalar(out=dst, in_=FA[:, 0:F],
                                                        scalar=THR[i], op=OP.is_gt)

            if B3 is not None and xeng == "gpsimd":
                padd = lambda dst, s1: (lambda e: e.tensor_tensor(
                    out=dst[:, 0:F], in0=dst[:, 0:F], in1=s1[:, 0:F], op=OP.add))
                p.emit("vector", cmp(0, B1[:, 0:F]), waits=(tok_z,), inc=dve)
                p.emit("vector", cmp(1, B2[:, 0:F]), inc=dve)
                a1 = p.emit("gpsimd", padd(B1, B2), waits=(dve,), inc="pool")
                p.emit("vector", cmp(2, B3[:, 0:F]), inc=dve)
                a2 = p.emit("gpsimd", padd(B1, B3), waits=(dve,), inc="pool")
                p.emit("vector", cmp(3, B2[:, 0:F]), waits=(a1,), inc=dve)
                a3 = p.emit("gpsimd", padd(B1, B2), waits=(dve,), inc="pool")
                p.emit("vector", cmp(4, B3[:, 0:F]), waits=(a2,), inc=dve)
                p.emit("vector", cmp(5, B2[:, 0:F]), waits=(a3,), inc=dve)
                a4 = p.emit("gpsimd", padd(B3, B2), waits=(dve,), inc="pool")
                p.emit("vector", cmp(6, B2[:, 0:F]), waits=(a4,), inc=dve)
                p.emit("vector", lambda e: e.scalar_tensor_tensor(out=B1[:, 0:F],
                       in0=B1[:, 0:F], scalar=0.5, in1=B3[:, 0:F], op0=OP.mult,
                       op1=OP.add), waits=(a3, a4), inc=dve)
                p.emit("vector", lambda e: e.scalar_tensor_tensor(out=B1[:, 0:F],
                       in0=B2[:, 0:F], scalar=2.0, in1=B1[:, 0:F], op0=OP.mult,
                       op1=OP.add), inc=dve)
            else:
                p.emit("vector", cmp(0, B1[:, 0:F]), waits=(tok_z,), inc=dve)
                for i in (1, 2, 3):
                    p.emit("vector", cmp(i, B2[:, 0:F]), inc=dve)
                    p.emit("vector", lambda e: e.tensor_tensor(out=B1[:, 0:F],
                           in0=B1[:, 0:F], in1=B2[:, 0:F], op=OP.add), inc=dve)
                p.emit("vector", lambda e: e.tensor_scalar_mul(B1[:, 0:F],
                       B1[:, 0:F], 0.5), inc=dve)
                for i in (4, 5):
                    p.emit("vector", cmp(i, B2[:, 0:F]), inc=dve)
                    p.emit("vector", lambda e: e.tensor_tensor(out=B1[:, 0:F],
                           in0=B1[:, 0:F], in1=B2[:, 0:F], op=OP.add), inc=dve)
                p.emit("vector", cmp(6, B2[:, 0:F]), inc=dve)
                p.emit("vector", lambda e: e.scalar_tensor_tensor(out=B1[:, 0:F],
                       in0=B2[:, 0:F], scalar=2.0, in1=B1[:, 0:F], op0=OP.mult,
                       op1=OP.add), inc=dve)
            if aeng == "scalar":
                tok_sgn = p.emit("scalar", lambda e: e.activation(out=FA[:, 0:F],
                       in_=src_f32, func=AF.Sign), waits=(dve,), inc=act)
            else:
                p.emit("vector", lambda e: e.tensor_single_scalar(out=FA[:, 0:F],
                       in_=src_f32, scalar=0.0, op=OP.is_gt), inc=dve)
                tok_sgn = p.emit("vector", lambda e: e.tensor_scalar(out=FA[:, 0:F],
                       in0=FA[:, 0:F], scalar1=2.0, scalar2=-1.0, op0=OP.mult,
                       op1=OP.add), inc=dve)
            sb_b = SBB[:, 176:176 + nb].unsqueeze(2).broadcast_to((128, nb, 16))
            tok_b2 = p.emit(xeng, lambda e: e.tensor_tensor(
                out=B2[:, 0:F].rearrange("p (b s) -> p b s", s=16),
                in0=B1[:, 0:F].rearrange("p (b s) -> p b s", s=16), in1=sb_b,
                op=OP.mult), waits=(dve,), inc=xinc)
            tok_d = p.emit("vector", lambda e: e.tensor_tensor(out=dst_bf16,
                   in0=B2[:, 0:F], in1=FA[:, 0:F], op=OP.mult),
                   waits=(tok_sgn, tok_b2) + tuple(dst_w), inc=dve)
            return tok_d, tok_sgn

        quant_g = lambda *a, **k: quant(*a, aeng="scalar", **k)[0]

        s12 = ExitStack()
        sb = lambda n, shp, dt: s12.enter_context(nc.sbuf_tensor(n, shp, dt))
        Mtri = sb("Mtri", [128, 512], BF16)
        OnesF = sb("OnesF", [1, 128], F32)
        BqkC = sb("BqkC", [128, 16], F32)
        BqkkR = sb("BqkkR", [1, D], F32)
        BqvR = sb("BqvR", [1, D], F32)
        BoR = sb("BoR", [1, D], F32)
        Gam = sb("Gam", [128, D], F32)
        Bet = sb("Bet", [128, D], F32)
        Wq = sb("Wq", [128, 8 * 3072], BF16)
        Wo = sb("Wo", [128, 8 * 1024], BF16)
        XN = sb("XN", [128, 2 * 1024], BF16)
        XNT = sb("XNT", [128, 8 * 512], BF16)
        QT = sb("QT", [128, 8 * 512], BF16)
        KT = sb("KT", [128, 8 * 512], BF16)
        KTM = sb("KTM", [128, 1024], BF16)
        Vp = sb("Vp", [128, 1040], BF16)
        STm = sb("STm", [128, 2048], BF16)
        ATT = sb("ATT", [128, 1024], BF16)
        KVs = sb("KVs", [128, 520], BF16)
        SST = sb("SST", [128, 520], F32)
        F1 = sb("F1", [128, 1024], F32)
        F2 = sb("F2", [128, 1024], F32)
        F3a = sb("F3a", [128, 512], F32)
        F3b = F2[:, 512:1024]
        DC = sb("DC", [128, 16], F32)
        XC = sb("XC", [128, 6 * 1024], F32)
        MCOL = sb("MCOL", [128, 1], F32)
        VCOL = sb("VCOL", [128, 1], F32)
        VC2 = sb("VC2", [128, 1], F32)
        T0C = sb("T0C", [128, 1], F32)
        T1C = sb("T1C", [128, 1], F32)
        mcol = MCOL[:, 0:1]; vcol = VCOL[:, 0:1]; vc2 = VC2[:, 0:1]
        t0c = T0C[:, 0:1]; t1c = T1C[:, 0:1]

        for (dst, src) in ((Mtri, d_mtri), (OnesF, d_ones), (BqkC, d_bqk_cols),
                           (BqkkR, d_bqkk_row), (BqvR, d_bqv_row), (BoR, d_bo_row),
                           (Gam, d_gamma), (Bet, d_beta)):
            p.emit("sync", lambda e, a=dst, b=src: e.dma_start(out=a[:, :], in_=b),
                   inc="ldx", k=16)
        for j in range(8):
            p.emit("gpsimd", lambda e, j=j: e.dma_start(
                out=Wq[:, j * 3072:(j + 1) * 3072],
                in_=d_wqkvT[128 * j:128 * (j + 1), :]), inc="ldw", k=16)
        for j in range(8):
            p.emit("gpsimd", lambda e, j=j: e.dma_start(
                out=Wo[:, j * 1024:(j + 1) * 1024],
                in_=d_woutT[128 * j:128 * (j + 1), :]), inc="ldw", k=16)
        TOK_CONST = (p.tok("ldx"), p.tok("ldw"))
        NBqkC = sb("NBqkC", [128, 16], F32)
        p.emit("vector", lambda e: e.tensor_scalar_mul(NBqkC[:, :], BqkC[:, :], -1.0),
               waits=(TOK_CONST[0],), inc=dve)

        p1_sq = [None]

        def ln_to(src_ap, dst_ap, extra=()):
            p.emit("vector", lambda e: e.reduce_sum(out=mcol, in_=src_ap, axis=AX.X),
                   waits=tuple(extra), inc=dve)
            p.emit("vector", lambda e: e.tensor_scalar_mul(mcol, mcol, 1.0 / D),
                   inc=dve)
            p.emit("vector", lambda e: e.tensor_scalar(out=F1[:, :], in0=src_ap,
                   scalar1=mcol, scalar2=None, op0=OP.subtract),
                   waits=(p1_sq[0],) if p1_sq[0] else (), inc=dve)
            p1_sq[0] = p.emit("scalar", lambda e: e.activation(out=F2[:, :],
                   in_=F1[:, :], func=AF.Square, accum_out=vcol),
                   waits=(dve,), inc=act)
            p.emit("vector", lambda e: e.tensor_scalar(out=vcol, in0=vcol,
                   scalar1=1.0 / D, scalar2=LN_EPS, op0=OP.mult, op1=OP.add),
                   waits=(act,), inc=dve)
            p.emit("scalar", lambda e: e.activation(out=vc2, in_=vcol, func=AF.Ln),
                   waits=(dve,), inc=act)
            p.emit("scalar", lambda e: e.activation(out=vc2, in_=vc2, func=AF.Exp,
                   scale=-0.5), inc=act)
            return p.emit("vector", lambda e: e.tensor_scalar(out=dst_ap,
                   in0=F1[:, :], scalar1=vc2, scalar2=None, op0=OP.mult),
                   waits=(act,), inc=dve)

        tp_free = [None]

        def transposes(src_slices, dst3, n, src_tok=None):
            for j in range(n):
                w = ()
                if j == 0:
                    w = (src_tok,) if src_tok else (pe, dve, act)
                    if tp_free[0]:
                        w = w + (tp_free[0],)
                p.emit("tensor", lambda e, j=j: e.transpose(
                    TP[:, 128 * j:128 * (j + 1)], src_slices(j), Id[:, :]),
                    waits=w, inc=pe)
            tp_free[0] = p.emit("scalar", lambda e: e.copy(
                out=dst3, in_=TP[:, 0:128 * n].rearrange("p (j t) -> p j t", t=128)),
                waits=(pe,), inc=act)
            return tp_free[0]

        def feature_evac(bank, dst_ap, bias_col=None, nbias_col=None, pe_tok=None):
            w0 = (pe_tok,) if pe_tok else (pe,)
            w0 = w0 + (dve,)
            if bias_col is None:
                p.emit("scalar", lambda e: e.activation(out=F3a[:, :], in_=bank[:, :],
                       func=AF.Relu, scale=-1.0), waits=w0, inc=act)
                tok_r2 = p.emit("scalar", lambda e: e.activation(out=F3b,
                       in_=bank[:, :], func=AF.Relu), inc=act)
            else:
                p.emit("scalar", lambda e: e.activation(out=F3a[:, :], in_=bank[:, :],
                       func=AF.Relu, scale=-1.0, bias=nbias_col),
                       waits=w0, inc=act)
                tok_r2 = p.emit("scalar", lambda e: e.activation(out=F3b,
                       in_=bank[:, :], func=AF.Relu, bias=bias_col), inc=act)
            p.emit("scalar", lambda e: e.activation(out=F2[:, 0:512], in_=F3a[:, :],
                   func=AF.Exp, scale=-1.0), inc=act)
            return p.emit("vector", lambda e: e.scalar_tensor_tensor(out=dst_ap,
                   in0=F3b, scalar=FEPS, in1=F2[:, 0:512], op0=OP.add, op1=OP.add),
                   waits=(act,), inc=dve)

        def state_slice(h):
            Sb = S0 if h < 8 else S1
            pr = (h // 2) % 4
            return Sb[64 * (h % 2):64 * (h % 2) + 64, 65 * pr:65 * pr + 65]

        def kvs_slice(h):
            off = 260 * (h // 8) + 65 * ((h // 2) % 4)
            return KVs[64 * (h % 2):64 * (h % 2) + 64, off:off + 65]

        s1x = ExitStack()
        sb1 = lambda n, shp, dt: s1x.enter_context(nc.sbuf_tensor(n, shp, dt))
        RW = sb1("RW", [128, 2 * 2048], F32)
        WQS = sb1("WQS", [128, 2048], BF16)
        W_FA = sb1("wFA", [128, 2048], BF16)
        W_B1 = sb1("wB1", [128, 2048], BF16)
        W_B2 = sb1("wB2", [128, 2048], BF16)
        W_MB = sb1("wMB", [128, 128], F32)
        W_SB = sb1("wSB", [128, 128], F32)
        W_T1 = sb1("wT1", [128, 128], F32)
        W_T2 = sb1("wT2", [128, 128], F32)
        SBBp = sb1("SBBp", [128, 304], BF16)
        W_DQ = sb1("wDQ", [128, 2 * 2048], BF16)
        

        p.emit("vector", lambda e: e.memset(SST[:, :], 0.0), inc=dve)
        p.emit("vector", lambda e: e.tensor_copy(
            out=Vp[:, :].rearrange("p (h f) -> p h f", f=65)[:, :, 64:65],
            in_=Mc[:, 0:1].unsqueeze(1).broadcast_to((128, 16, 1))),
            waits=(TOK_CONST[0],), inc=dve)

        NWP = 38
        wpairs = []
        for it in range(11):
            wpairs.append(([(d_wg[256 * it:256 * it + 128, :], 0, 1024),
                            (d_wg[256 * it + 128:256 * it + 256, :], 1024, 1024)],
                           [(2 * it, 0, 1024), (2 * it + 1, 1024, 1024)], 2048, 16))
        for it in range(11):
            wpairs.append(([(d_wu[256 * it:256 * it + 128, :], 0, 1024),
                            (d_wu[256 * it + 128:256 * it + 256, :], 1024, 1024)],
                           [(22 + 2 * it, 0, 1024), (23 + 2 * it, 1024, 1024)],
                           2048, 16))
        for kt in range(8):
            for half in range(2):
                wpairs.append(([(d_wd[128 * kt:128 * (kt + 1),
                                      1408 * half:1408 * (half + 1)], 0, 1408)],
                               [(44 + 2 * kt + half, 0, 1408)], 1408, 11))
        w_load_tok = [None] * NWP
        w_q_tok = [None, None]
        w_out_tok = [None, None]
        w_last_out = [None]
        w_last_wqevac = [None]

        tk0 = None
        for (s_, off, wd_) in wpairs[0][0]:
            tk0 = p.emit("sync", lambda e, b=RW[:, off:off + wd_], s=s_:
                e.dma_start(out=b, in_=s), waits=() if tk0 is None else (tk0,),
                inc="lds", k=16)
        w_load_tok[0] = tk0

        w_tp_pe = [None, None]

        def wquant_q(i):
            (loads, stores, F, ntp) = wpairs[i]
            slot = 2048 * (i % 2)
            if i + 1 < NWP:
                nslot = 2048 * ((i + 1) % 2)
                w = (w_load_tok[i],)
                if w_q_tok[(i + 1) % 2]:
                    w = w + (w_q_tok[(i + 1) % 2],)
                if w_tp_pe[(i + 1) % 2]:
                    w = w + (w_tp_pe[(i + 1) % 2],)
                tk = None
                for (s_, off, wd_) in wpairs[i + 1][0]:
                    tk = p.emit("sync", lambda e,
                        b=RW[:, nslot + off:nslot + off + wd_], s=s_:
                        e.dma_start(out=b, in_=s),
                        waits=w if tk is None else (tk,), inc="lds", k=16)
                w_load_tok[i + 1] = tk
            wq_w0 = (w_load_tok[i],)
            if w_q_tok[(i + 1) % 2]:
                wq_w0 = wq_w0 + (w_q_tok[(i + 1) % 2],)
            dstv = W_DQ[:, 2048 * (i % 2):2048 * (i % 2) + 2048]
            dw = (w_tp_pe[i % 2],) if w_tp_pe[i % 2] else ("pe",)
            w_q_tok[i % 2] = quant_g(RW[:, slot:slot + F], F,
                                   dstv[:, 0:F], W_FA, W_B1, W_B2,
                                   W_MB, W_SB, W_T1, W_T2, SBBp,
                                   w0=wq_w0, dst_w=dw)

        def wquant_tp(i):
            (loads, stores, F, ntp) = wpairs[i]
            slot = 2048 * (i % 2)
            dstv = W_DQ[:, 2048 * (i % 2):2048 * (i % 2) + 2048]
            wq = WQS[:, 0:2048]
            done = 0
            evac_tok = None
            while done < ntp:
                nb_t = min(8, ntp - done)
                for j in range(nb_t):
                    w = ()
                    if j == 0:
                        w = (w_q_tok[i % 2],)
                        if tp_free[0] is not None:
                            w = w + (tp_free[0],)
                        if done == 0 and w_last_out[0] is not None:
                            w = w + (w_last_out[0],)
                    p.emit("tensor", lambda e, j=j, d0=done, dstv=dstv:
                        e.transpose(
                        TP[:, 128 * j:128 * (j + 1)],
                        dstv[:, 128 * (d0 + j):128 * (d0 + j + 1)], Id[:, :]),
                        waits=w, inc=pe)
                evac_tok = p.emit("scalar", lambda e, d0=done, n=nb_t, wq=wq:
                    e.copy(
                    out=wq[:, 128 * d0:128 * (d0 + n)],
                    in_=TP[:, 0:128 * n].rearrange("p (j t) -> p j t", t=128)),
                    waits=(pe,), inc=act)
                tp_free[0] = evac_tok
                done += nb_t
            w_last_wqevac[0] = evac_tok
            w_tp_pe[i % 2] = p.tok(pe)
            tk = None
            for (di, off, wd_) in stores:
                w = (evac_tok,) if tk is None else (tk,)
                if tk is None and w_last_out[0] is not None:
                    w = w + (w_last_out[0],)
                tk = p.emit("gpsimd", lambda e, di=di, off=off, wd_=wd_, wq=wq:
                    e.dma_start(out=d_wq[di, :, 0:wd_],
                                in_=wq[:, off:off + wd_]),
                    waits=w, inc="wout", k=16)
            w_out_tok[i % 2] = tk
            w_last_out[0] = tk

        NPREF = 0 if STOP_AFTER == "p0" else (1 if STOP_AFTER == "p1n1" else NCH)
        xp_tok = [None] * NCH
        xp_tok[0] = p.emit("sync", lambda e: e.dma_start(
            out=XC[:, 0:1024], in_=d_xp[0:128, :]), inc="ldp0", k=16)

        wq_i = [0]

        def emit_wquant_upto(limit):
            while wq_i[0] < min(limit, NWP):
                i = wq_i[0]
                if i >= 1:
                    wquant_tp(i - 1)
                wquant_q(i)
                wq_i[0] += 1

        bank_free = [None, None]
        bank_free_v = [None, None]
        sst_free = [None]
        tok_state_pe = [None]
        for c in range(NPREF):
            xpb = XC[:, 1024 * (c % 2):1024 * (c % 2) + 1024]
            if c + 1 < NCH:
                nbuf = XC[:, 1024 * ((c + 1) % 2):1024 * ((c + 1) % 2) + 1024]
                src = d_xp[128 * (c + 1):128 * (c + 2), :]
                xp_tok[c + 1] = p.emit("sync", lambda e, a=nbuf, b=src: e.dma_start(
                    out=a, in_=b), waits=(dve, xp_tok[c]), inc="ldp0" if (c + 1) % 2 == 0
                    else "ldp1", k=16)
            xnb = XN[:, 1024 * (c % 2):1024 * (c % 2) + 1024]
            tok_xn = ln_to(xpb, xnb, extra=(xp_tok[c],))
            tok_xnt = transposes(lambda j, xnb=xnb: xnb[:, 128 * j:128 * (j + 1)],
                       XNT[:, :].rearrange("p (j t) -> p j t", t=512)
                       [:, :, 128 * (c % 2):128 * (c % 2) + 128], 8,
                       src_tok=tok_xn)
            if NPREF == NCH:
                emit_wquant_upto(int((c + 0.34) * NWP / NCH))
            tok_kv_pe = [None, None]
            for kind in range(2):
                for n in range(2):
                    bank = G[n] if kind == 0 else G[3 + n]
                    bfree = bank_free if kind == 0 else bank_free_v
                    coff = 1024 * (kind + 1) + 512 * n
                    for j in range(8):
                        w = ()
                        if j == 0:
                            w = (tok_xnt,) + TOK_CONST
                            if bfree[n]:
                                w = w + (bfree[n],)
                        p.emit("tensor", lambda e, j=j, b=bank, co=coff, c=c:
                            e.matmul(
                            b[:, :], XNT[:, 512 * j + 128 * (c % 2):
                                         512 * j + 128 * (c % 2) + 128],
                            Wq[:, 3072 * j + co:3072 * j + co + 512],
                            start=(j == 0), stop=False),
                            waits=w, inc=pe)
                    brow = BqkkR if kind == 0 else BqvR
                    tok_kv_pe[n] = p.emit("tensor", lambda e, b=bank, r=brow, n=n:
                        e.matmul(b[:, :], OnesF[0:1, 0:128],
                                 r[0:1, 512 * n:512 * n + 512],
                                 start=False, stop=True), inc=pe)
                if kind == 0:
                    for n in range(2):
                        bank_free[n] = feature_evac(G[n],
                            KTM[:, 512 * n:512 * n + 512], pe_tok=tok_kv_pe[n])
                    if NPREF == NCH:
                        emit_wquant_upto(int((c + 0.67) * NWP / NCH))
                else:
                    for n in range(2):
                        dst = Vp[:, 520 * n:520 * n + 520].rearrange(
                            "p (h f) -> p h f", f=65)[:, :, 0:64]
                        bank_free_v[n] = p.emit("vector", lambda e, b=G[3 + n], d=dst:
                            e.tensor_scalar(
                            out=d, in0=b[:, :].rearrange("p (h f) -> p h f", f=64),
                            scalar1=Mc[:, 0:1], scalar2=None, op0=OP.mult),
                            waits=(tok_kv_pe[n],), inc=dve)
            for grp in range(2):
                first = True
                for idx in range(8):
                    h = 2 * idx + grp
                    w = ()
                    if first:
                        w = (bank_free[0], bank_free[1],
                             bank_free_v[0], bank_free_v[1])
                        if sst_free[0]:
                            w = w + (sst_free[0],)
                    p.emit("tensor", lambda e, h=h, idx=idx: e.matmul(
                        state_slice(h), KTM[:, 64 * h:64 * h + 64],
                        Vp[:, 65 * h:65 * h + 65], start=(idx in (0, 4)),
                        stop=(idx in (3, 7)),
                        tile_position=(0, 64 * (h % 2))),
                        waits=w, inc=pe)
                    first = False
            tok_state_pe[0] = p.tok(pe)
            p.emit("vector", lambda e: e.tensor_tensor(out=SST[:, 0:260],
                   in0=SST[:, 0:260], in1=S0[:, 0:260], op=OP.add),
                   waits=(tok_state_pe[0],), inc=dve)
            sst_free[0] = p.emit("vector", lambda e: e.tensor_tensor(
                   out=SST[:, 260:520],
                   in0=SST[:, 260:520], in1=S1[:, 0:260], op=OP.add), inc=dve)
            emit_wquant_upto((c + 1) * NWP // NCH if NPREF == NCH else 0)

        emit_wquant_upto(NWP)
        wquant_tp(NWP - 1)
        p.join()
        p.flush(block)
        s1x.close()
        s2q = ExitStack()
        sb2 = lambda n, shp, dt: s2q.enter_context(nc.sbuf_tensor(n, shp, dt))
        XQN = sb2("XQN", [128, 1024], F32)
        AO = sb2("AO", [128, 1024], F32)
        if STOP_AFTER in ("p0", "p1", "p1n1"):
            p.emit("vector", lambda e: e.tensor_copy(out=F1[:, 0:520], in_=SST[:, :]),
                   waits=(pe,), inc=dve)
            p.emit("gpsimd", lambda e: e.dma_start(out=d_y[0:128, :],
                   in_=F1[:, :]), waits=(dve, act, pe), inc="out", k=16)
            p.emit("gpsimd", lambda e: e.engine_nop(), waits=("out",))
            p.flush(block)
            s12.close()
            return nc

        p.emit("vector", lambda e: e.memset(
            Vp[:, :].rearrange("p (h f) -> p h f", f=65)[:, :, 64:65], 1.0), inc=dve)
        xm_tok = [None] * NCH
        xm_tok[0] = p.emit("sync", lambda e: e.dma_start(
            out=XC[:, 0:1024], in_=d_xm[0:128, :]), waits=("ldp0", "ldp1", dve),
            inc="ldp0", k=16)

        fe_tok = [None, None, None, None]
        fe_last = [None]
        xnt_tok = [None]

        def qkv_stage(g):
            for c4 in range(4):
                c = 4 * g + c4
                if c + 1 < NCH:
                    nbuf = XC[:, 1024 * ((c + 1) % 6):1024 * ((c + 1) % 6) + 1024]
                    src = d_xm[128 * (c + 1):128 * (c + 2), :]
                    xm_tok[c + 1] = p.emit("sync", lambda e, a=nbuf, b=src:
                        e.dma_start(out=a, in_=b), waits=(dve, "out", xm_tok[c]),
                        inc="ldp0" if (c + 1) % 2 == 0 else "ldp1", k=16)
                ln_to(XC[:, 1024 * (c % 6):1024 * (c % 6) + 1024], XN[:, 0:1024],
                      extra=(xm_tok[c],))
                xnt_tok[0] = transposes(lambda j: XN[:, 128 * j:128 * (j + 1)],
                           XNT[:, :].rearrange("p (j t) -> p j t", t=512)
                           [:, :, 128 * c4:128 * c4 + 128], 8)
            for ct in range(16):
                bank = G[ct % 4]
                for j in range(8):
                    w = ()
                    if j == 0:
                        w = (fe_tok[ct % 4],) if fe_tok[ct % 4] else (dve, act)
                        if ct == 0:
                            w = w + (xnt_tok[0],)
                    p.emit("tensor", lambda e, j=j, b=bank, ct=ct: e.matmul(
                        b[:, :], Wq[:, 3072 * j + 128 * ct:3072 * j + 128 * ct + 128],
                        XNT[:, 512 * j:512 * (j + 1)], start=(j == 0), stop=(j == 7)),
                        waits=w, inc=pe)
                dst = (QT if ct < 8 else KT)[:, 512 * (ct % 8):512 * (ct % 8) + 512]
                fe_tok[ct % 4] = feature_evac(bank, dst,
                    bias_col=BqkC[:, ct:ct + 1], nbias_col=NBqkC[:, ct:ct + 1])
            fe_last[0] = fe_tok[3]

        ao_ev = [None, None]

        def ln2_epilogue(c):
            p.emit("vector", lambda e: e.reduce_sum(out=mcol, in_=AO[:, :],
                   axis=AX.X), waits=(ao_ev[1],), inc=dve)
            p.emit("vector", lambda e: e.tensor_scalar_mul(mcol, mcol, 1.0 / D),
                   inc=dve)
            p.emit("vector", lambda e: e.tensor_scalar(
                out=F1[:, :], in0=AO[:, :], scalar1=mcol,
                scalar2=None, op0=OP.subtract),
                waits=(p1_sq[0],) if p1_sq[0] else (), inc=dve)
            p1_sq[0] = p.emit("scalar", lambda e: e.activation(out=F2[:, :],
                   in_=F1[:, :], func=AF.Square, accum_out=vcol),
                   waits=(dve,), inc=act)
            p.emit("vector", lambda e: e.tensor_scalar(out=vcol, in0=vcol,
                   scalar1=1.0 / D, scalar2=LN_EPS, op0=OP.mult, op1=OP.add),
                   waits=(act,), inc=dve)
            p.emit("scalar", lambda e: e.activation(out=vc2, in_=vcol, func=AF.Ln),
                   waits=(dve,), inc=act)
            p.emit("scalar", lambda e: e.activation(out=vc2, in_=vc2, func=AF.Exp,
                   scale=-0.5), inc=act)
            p.emit("vector", lambda e: e.scalar_tensor_tensor(out=F2[:, :],
                   in0=F1[:, :], scalar=vc2, in1=Gam[:, :], op0=OP.mult, op1=OP.mult),
                   waits=(act,), inc=dve)
            p.emit("vector", lambda e: e.tensor_tensor(out=F2[:, :], in0=F2[:, :],
                   in1=Bet[:, :], op=OP.add), inc=dve)
            xc = XC[:, 1024 * (c % 6):1024 * (c % 6) + 1024]
            p.emit("vector", lambda e, xc=xc: e.tensor_tensor(out=xc, in0=xc,
                   in1=F2[:, :], op=OP.add), inc=dve)
            p.emit("gpsimd", lambda e, c=c, xc=xc: e.dma_start(
                out=d_y[128 * c:128 * (c + 1), :], in_=xc), waits=(dve,),
                inc="out", k=16)

        def ffn_quant(c):
            xq_src = XC[:, 1024 * (c % 6):1024 * (c % 6) + 1024]
            p.emit("vector", lambda e, s=xq_src: e.reduce_sum(out=mcol, in_=s,
                   axis=AX.X), inc=dve)
            p.emit("vector", lambda e: e.tensor_scalar_mul(mcol, mcol, 1.0 / D),
                   inc=dve)
            p.emit("vector", lambda e, s=xq_src: e.tensor_scalar(out=F1[:, :],
                   in0=s, scalar1=mcol, scalar2=None, op0=OP.subtract),
                   waits=(p1_sq[0],) if p1_sq[0] else (), inc=dve)
            p1_sq[0] = p.emit("scalar", lambda e: e.activation(out=F2[:, :],
                   in_=F1[:, :], func=AF.Square, accum_out=vcol),
                   waits=(dve,), inc=act)
            p.emit("vector", lambda e: e.tensor_scalar(out=vcol, in0=vcol,
                   scalar1=1.0 / D, scalar2=LN_EPS, op0=OP.mult, op1=OP.add),
                   waits=(act,), inc=dve)
            p.emit("scalar", lambda e: e.activation(out=vc2, in_=vcol, func=AF.Ln),
                   waits=(dve,), inc=act)
            p.emit("scalar", lambda e: e.activation(out=vc2, in_=vc2, func=AF.Exp,
                   scale=-0.5), inc=act)
            t_f2 = p.emit("vector", lambda e: e.tensor_scalar(out=XQN[:, :],
                   in0=F1[:, :], scalar1=vc2, scalar2=None, op0=OP.mult),
                   waits=(act, "out"), inc=dve)
            p.emit("gpsimd", lambda e, c=c: e.dma_start(
                out=d_xs[128 * c:128 * (c + 1), :], in_=XQN[:, :]),
                waits=(t_f2,), inc="out", k=16)

        att_tok = [None]
        p2_sst = [None]

        def attn_chunk(g, c4, upto="zz"):
            c = 4 * g + c4
            v_ev = [None, None]
            stm_toks = []
            for n in range(2):
                bank = G[n]
                for j in range(8):
                    w = ()
                    if j == 0:
                        if ao_ev[0]:
                            w = (ao_ev[0], ao_ev[1])
                        else:
                            w = (dve, act)
                        if c4 == 0:
                            w = w + (fe_tok[0], fe_tok[1], xnt_tok[0])
                    p.emit("tensor", lambda e, j=j, b=bank, n=n, c4=c4: e.matmul(
                        b[:, :], XNT[:, 512 * j + 128 * c4:512 * j + 128 * c4 + 128],
                        Wq[:, 3072 * j + 2048 + 512 * n:3072 * j + 2048 + 512 * n + 512],
                        start=(j == 0), stop=False),
                        waits=w, inc=pe)
                p.emit("tensor", lambda e, b=bank, n=n: e.matmul(
                    b[:, :], OnesF[0:1, 0:128], BqvR[0:1, 512 * n:512 * n + 512],
                    start=False, stop=True), inc=pe)
                dst = Vp[:, 520 * n:520 * n + 520].rearrange(
                    "p (h f) -> p h f", f=65)[:, :, 0:64]
                v_ev[n] = p.emit("vector", lambda e, b=bank, d=dst: e.tensor_copy(
                    out=d, in_=b[:, :].rearrange("p (h f) -> p h f", f=64)),
                    waits=(pe,), inc=dve)
            if upto == "v": return
            tok_ktm = transposes(lambda j, c4=c4: KT[:, 512 * j + 128 * c4:512 * j + 128 * c4 + 128],
                       KTM[:, :].rearrange("p (j t) -> p j t", t=128), 8,
                       src_tok=fe_last[0])
            if upto == "ktm": return
            kvs_tok = p.emit("vector", lambda e: e.tensor_copy(out=KVs[:, :],
                   in_=SST[:, :]), inc=dve)
            if upto == "kvs": return
            for grp in range(2):
                first = True
                for half in range(2):
                    bank = G[half]
                    for a in range(4):
                        h = 8 * half + 2 * a + grp
                        bp = 64 * grp
                        kap = KT[bp:bp + 64,
                                 512 * (h // 2) + 128 * c4:512 * (h // 2) + 128 * c4 + 128]
                        qap = QT[bp:bp + 64,
                                 512 * (h // 2) + 128 * c4:512 * (h // 2) + 128 * c4 + 128]
                        w = ()
                        if first:
                            w = (pe, v_ev[0], v_ev[1]) + tuple(stm_toks)
                        p.emit("tensor", lambda e, b=bank, a=a, kap=kap, qap=qap, bp=bp:
                               e.matmul(b[:, 128 * a:128 * a + 128], kap, qap,
                                        start=(a == 0), stop=(a == 3),
                                        tile_position=(bp, 0)),
                               waits=w, inc=pe)
                        first = False
                if upto != "stmm":
                    for half in range(2):
                        dst = STm[:, 1024 * half:1024 * half + 1024].rearrange(
                            "p (a r) -> p a r", r=256)[:, :, 128 * grp:128 * grp + 128]
                        src3 = G[half][:, :].rearrange("p (a t) -> p a t", t=128)
                        mt3 = Mtri[:, :].rearrange("p (a t) -> p a t", t=128)
                        stm_toks.append(p.emit("vector", lambda e, d=dst, s=src3,
                            m=mt3: e.tensor_tensor(
                            out=d, in0=s, in1=m, op=OP.mult), waits=(pe,), inc=dve))
            if upto in ("st", "stmm"): return
            if c >= 1:
                ln2_epilogue(c - 1)
                ffn_quant(c - 1)
            def nbank(h):
                b = G[2] if h < 7 else (G[3] if h < 14 else G[4])
                return b, 65 * (h - (0 if h < 7 else (7 if h < 14 else 14)))
            for h in range(H):
                b, c0 = nbank(h)
                p.emit("tensor", lambda e, b=b, c0=c0, h=h: e.matmul(
                    b[:, c0:c0 + 65], STm[:, 128 * h:128 * h + 128],
                    Vp[:, 65 * h:65 * h + 65], start=(h in (0, 7, 14)), stop=False),
                    waits=(pe,) + tuple(stm_toks) if h == 0 else (), inc=pe)
            for grp in range(2):
                first = True
                for idx in range(8):
                    h = 2 * idx + grp
                    b, c0 = nbank(h)
                    bp = 64 * grp
                    qap = QT[bp:bp + 64,
                             512 * (h // 2) + 128 * c4:512 * (h // 2) + 128 * c4 + 128]
                    p.emit("tensor", lambda e, b=b, c0=c0, qap=qap, h=h, bp=bp:
                           e.matmul(b[:, c0:c0 + 65], qap, kvs_slice(h),
                                    start=False, stop=(h in (5, 13, 15)),
                                    tile_position=(bp, 0)),
                           waits=(pe, kvs_tok) if first else (), inc=pe)
                    first = False
            if upto == "num": return
            for (bi, h0, k) in ((2, 0, 7), (3, 7, 7), (4, 14, 2)):
                bank = G[bi]
                dcv = DC[:, h0:h0 + k]
                b3 = bank[:, 0:65 * k].rearrange("p (h f) -> p h f", f=65)
                p.emit("vector", lambda e, b3=b3, d=dcv, k=k: e.tensor_scalar(
                    out=d, in0=b3[:, :, 64], scalar1=DEN_EPS, scalar2=None, op0=OP.add),
                    waits=(pe,), inc=dve)
                p.emit("vector", lambda e, d=dcv: e.reciprocal(out=d, in_=d), inc=dve)
                att3 = ATT[:, 64 * h0:64 * (h0 + k)].rearrange("p (h f) -> p h f", f=64)
                att_tok[0] = p.emit("vector", lambda e, b3=b3, a3=att3, d=dcv, k=k:
                    e.tensor_tensor(
                    out=a3, in0=b3[:, :, 0:64],
                    in1=d.unsqueeze(2).broadcast_to((128, k, 64)), op=OP.mult), inc=dve)
            if upto == "div": return
            for grp in range(2):
                first = True
                for idx in range(8):
                    h = 2 * idx + grp
                    p.emit("tensor", lambda e, h=h, idx=idx: e.matmul(
                        state_slice(h), KTM[:, 64 * h:64 * h + 64],
                        Vp[:, 65 * h:65 * h + 65], start=(idx in (0, 4)),
                        stop=(idx in (3, 7)),
                        tile_position=(0, 64 * (h % 2))),
                        waits=((pe, tok_ktm) + ((p2_sst[0],) if p2_sst[0] else ()))
                        if first else (), inc=pe)
                    first = False
            p.emit("vector", lambda e: e.tensor_tensor(out=SST[:, 0:260],
                   in0=SST[:, 0:260], in1=S0[:, 0:260], op=OP.add),
                   waits=(pe,), inc=dve)
            p2_sst[0] = p.emit("vector", lambda e: e.tensor_tensor(
                   out=SST[:, 260:520],
                   in0=SST[:, 260:520], in1=S1[:, 0:260], op=OP.add), inc=dve)
            if upto == "state": return
            att_ev = transposes(lambda j: ATT[:, 128 * j:128 * (j + 1)],
                       ATT[:, :].rearrange("p (j t) -> p j t", t=128), 8,
                       src_tok=att_tok[0])
            if upto == "att": return
            for n in range(2):
                bank = G[n]
                for j in range(8):
                    p.emit("tensor", lambda e, j=j, b=bank, n=n: e.matmul(
                        b[:, :], ATT[:, 128 * j:128 * j + 128],
                        Wo[:, 1024 * j + 512 * n:1024 * j + 512 * n + 512],
                        start=(j == 0), stop=False),
                        waits=(att_ev,) if j == 0 else (), inc=pe)
                p.emit("tensor", lambda e, b=bank, n=n: e.matmul(
                    b[:, :], OnesF[0:1, 0:128], BoR[0:1, 512 * n:512 * n + 512],
                    start=False, stop=True), inc=pe)
            if upto == "proj": return
            for n in range(2):
                ao_ev[n] = p.emit("vector", lambda e, n=n: e.tensor_copy(
                    out=AO[:, 512 * n:512 * n + 512], in_=G[n][:, :]),
                    waits=(pe,) if n == 0 else (), inc=dve)

        if STOP_AFTER in ("p2qkv", "p2c1"):
            qkv_stage(0)
            if STOP_AFTER == "p2c1":
                attn_chunk(0, 0, upto=UPTO)
                for (i, ap) in enumerate((XN[:, :], QT[:, 0:1024], KT[:, 0:1024],
                                          KTM[:, :], Vp[:, 0:1024], STm[:, 0:1024],
                                          ATT[:, :], KVs[:, 0:520])):
                    p.emit("gpsimd", lambda e, i=i, ap=ap: e.dma_start(
                        out=d_y[128 * i:128 * i + 128, 0:ap.shape[1]], in_=ap),
                        waits=(dve, act, pe), inc="out", k=16)
        else:
            for g in range(4):
                qkv_stage(g)
                for c4 in range(4):
                    attn_chunk(g, c4)
            ln2_epilogue(NCH - 1)
            ffn_quant(NCH - 1)

        p.join()
        if STOP_AFTER in ("p2", "p2qkv", "p2c1"):
            p.emit("gpsimd", lambda e: e.engine_nop(), waits=("out",))
            p.flush(block)
            s12.close()
            return nc
        P2END = (p.tok(pe), p.tok(dve), p.tok(act), p.tok("pool"))
        p.flush(block)
        s2q.close()
        s12.close()

        s3 = ExitStack()
        sb3 = lambda n, shp, dt: s3.enter_context(nc.sbuf_tensor(n, shp, dt))
        XQT = sb3("XQT", [128, 8 * 512], BF16)
        WG = sb3("WG", [128, 22 * 1024], BF16)
        WU = sb3("WU", [128, 22 * 1024], BF16)
        WD = sb3("WD", [128, 16 * 1408], BF16)
        SBB = sb3("SBB", [128, 2 * 176], BF16)
        Q_XB = sb3("qXB", [128, 2 * 1024], F32)
        Q_XQ = sb3("qXQ", [128, 1024], BF16)
        FAx = sb3("FAx", [128, 2816], BF16)
        B1x = sb3("B1x", [128, 2816], BF16)
        B2x = sb3("B2x", [128, 2816], BF16)
        MBx = sb3("MBx", [128, 176], F32)
        SBx = sb3("SBx", [128, 176], F32)
        T1x = sb3("T1x", [128, 176], F32)
        T2x = sb3("T2x", [128, 176], F32)
        F_H = sb3("FH", [128, 2816], F32)
        HQ = sb3("HQ", [128, 2816], BF16)
        HQT = sb3("HQT", [128, 22 * 128], BF16)
        F2s = sb3("F2s", [128, 1024], BF16)
        SUb = sb3("SUb", [128, 1024], BF16)
        EV = sb3("EV", [128, 1024], F32)

        chains = ["lds", "wl2", "ldx", "wl3", "wl4", "wl5", "wl6", "wl7",
                  "wl8", "wl9", "wlA", "wlB", "wlC", "wlD", "wlE", "wlF"]
        ch_tok = [None] * 16
        for i in range(60):
            if i < 22:
                dst = WG[:, 1024 * i:1024 * (i + 1)]
                w_ = 1024
                k4 = i % 12
            elif i < 44:
                dst = WU[:, 1024 * (i - 22):1024 * (i - 21)]
                w_ = 1024
                k4 = (i - 22) % 12
            else:
                dst = WD[:, 1408 * (i - 44):1408 * (i - 43)]
                w_ = 1408
                k4 = 12 + (i - 44) % 4
            w = ("wout",) if ch_tok[k4] is None else (ch_tok[k4],)
            ch_tok[k4] = p.emit("sync", lambda e, d=dst, w_=w_, i=i: e.dma_start(
                out=d[:, 0:w_], in_=d_wq[i, :, 0:w_]), waits=w,
                inc=chains[k4], k=16)
        TOK_WLOAD = tuple(ch_tok[0:12])
        TOK_WDLOAD = tuple(ch_tok[12:16])

        prev_xqtp = [None]
        htp_tok = [None]
        dn_toks = [None, None]
        ev_toks = [None, None]
        hq_toks = [None] * NCH
        last_q = [None]
        xqe_tok = [None] * NCH
        xq_tok = [None] * (NCH + 1)
        xq_tok[0] = p.emit("sync", lambda e: e.dma_start(
            out=Q_XB[:, 0:1024], in_=d_xs[0:128, :]), waits=("out",),
            inc="ldp0", k=16)

        def xload(c):
            if c >= NCH:
                return
            nbuf = Q_XB[:, 1024 * (c % 2):1024 * (c % 2) + 1024]
            s_ = d_xs[128 * c:128 * (c + 1), :]
            w = (xq_tok[c - 1],)
            if prev_xqtp[0]:
                w = w + (prev_xqtp[0],)
            xq_tok[c] = p.emit("sync", lambda e, a=nbuf, b=s_: e.dma_start(
                out=a, in_=b), waits=w,
                inc="ldp0" if c % 2 == 0 else "ldp1", k=16)

        xq_dst = [None]

        def xquant_dve(c):
            w0 = (xq_tok[c],)
            if last_q[0]:
                w0 = w0 + (last_q[0],)
            if c == 0:
                w0 = w0 + (P2END[1], P2END[2])
            dw = (prev_xqtp[0],) if prev_xqtp[0] else (P2END[0],)
            xq_dst[0] = quant(Q_XB[:, 1024 * (c % 2):1024 * (c % 2) + 1024],
                              1024, Q_XQ[:, :], FAx, B1x, B2x,
                              MBx, SBx, T1x, T2x, SBB, w0=w0, dst_w=dw)[0]
            last_q[0] = xq_dst[0]

        def xq_tposes(c):
            xqe_tok[c] = transposes(
                lambda j: Q_XQ[:, 128 * j:128 * (j + 1)],
                XQT[:, :].rearrange("p (j t) -> p j t", t=512)
                [:, :, 128 * (c % 4):128 * (c % 4) + 128], 8,
                src_tok=xq_dst[0])
            prev_xqtp[0] = p.tok(pe)

        ITILES = [(0, 512), (512, 512), (1024, 512), (1536, 512), (2048, 512),
                  (2560, 256)]
        tok_actfree = [None, None]
        tok_fh_par = [None, None]
        tok_fh2 = [None]
        tok_poolFH = [None]
        WG3 = WG[:, :].rearrange("p (t j r) -> p t j r", j=8, r=128)
        WU3 = WU[:, :].rearrange("p (t j r) -> p t j r", j=8, r=128)
        WD3 = WD[:, :].rearrange("p (k c) -> p k c", c=2816)

        def gates_silu(c):
            for ti, (i0_, ilen) in enumerate(ITILES):
                it0 = i0_ // 128
                nt = ilen // 128
                par = ti % 2
                Gg, Gu = G[2 * par], G[2 * par + 1]
                for (bank, Wt3) in ((Gg, WG3), (Gu, WU3)):
                    for j in range(8):
                        w = ()
                        if j == 0 and bank is Gg:
                            if tok_actfree[par]:
                                w = (tok_actfree[par],)
                            else:
                                w = P2END[:3] + TOK_WLOAD
                            if ti == 0:
                                w = w + (xqe_tok[c],)
                        p.emit("tensor", lambda e, j=j, b=bank, W3=Wt3, it0=it0,
                               nt=nt, il=ilen, c=c: e.matmul(
                               b[:, 0:il],
                               XQT[:, 512 * j + 128 * (c % 4):512 * j + 128 * (c % 4) + 128],
                               W3[:, it0:it0 + nt, j, :],
                               start=(j == 0), stop=(j == 7)),
                               waits=w, inc=pe)
                tok_pe_gates = p.tok(pe)
                w_sil = (tok_pe_gates,)
                w_sil += (tok_fh_par[par],) if tok_fh_par[par] else (P2END[3],)
                p.emit("scalar", lambda e, il=ilen, Gg=Gg, par=par:
                    e.activation(out=F2s[:, 512 * par:512 * par + il],
                    in_=Gg[:, 0:il], func=AF.Silu),
                    waits=w_sil, inc=act)
                tok_su = p.emit("scalar", lambda e, il=ilen, Gu=Gu, par=par: e.copy(
                    out=SUb[:, 512 * par:512 * par + il], in_=Gu[:, 0:il]), inc=act)
                w_fh = ()
                if i0_ == 0:
                    w_fh = hq_toks[c - 1] if c >= 1 else (P2END[1], P2END[2])
                tk = p.emit("gpsimd", lambda e, i0_=i0_, il=ilen, par=par:
                    e.tensor_tensor(
                    out=F_H[:, i0_:i0_ + il], in0=F2s[:, 512 * par:512 * par + il],
                    in1=SUb[:, 512 * par:512 * par + il],
                    op=OP.mult), waits=(tok_su,) + w_fh, inc="pool")
                tok_actfree[par] = tok_su
                tok_fh_par[par] = tk
                if ti == 2:
                    tok_fh2[0] = tk
                if i0_ == ITILES[-1][0]:
                    tok_poolFH[0] = tk

        def tposes_down(c):
            for half in range(2):
                done = 0
                while done < 11:
                    nb_t = min(8, 11 - done)
                    for j in range(nb_t):
                        w = ()
                        if j == 0:
                            if tp_free[0]:
                                w = (tp_free[0],)
                            if done == 0 and half == 0:
                                w = w + hq_toks[c]
                        p.emit("tensor", lambda e, j=j, d0=done, half=half: e.transpose(
                            TP[:, 128 * j:128 * (j + 1)],
                            HQ[:, 1408 * half + 128 * (d0 + j):
                                1408 * half + 128 * (d0 + j + 1)], Id[:, :]),
                            waits=w, inc=pe)
                    tp_free[0] = p.emit("scalar", lambda e, d0=done, n=nb_t, half=half:
                        e.copy(
                        out=HQT[:, :].rearrange("p (j t) -> p j t", t=128)
                        [:, 11 * half + d0:11 * half + d0 + n, :],
                        in_=TP[:, 0:128 * n].rearrange("p (j t) -> p j t", t=128)),
                        waits=(pe,), inc=act)
                    done += nb_t
            htp_tok[0] = p.tok(pe)
            for n in range(2):
                bank = (S0, S1)[n]
                for it in range(22):
                    half = it // 11
                    jj = it % 11
                    off = 1408 * half + 128 * jj
                    rhs = WD3[:, 4 * n:4 * n + 4, off:off + 128]
                    w = ()
                    if n == 0 and it == 0:
                        w = (tp_free[0],)
                        if ev_toks[0]:
                            w = w + (ev_toks[0], ev_toks[1])
                        else:
                            w = w + (P2END[1], P2END[2]) + TOK_WDLOAD
                    p.emit("tensor", lambda e, it=it, b=bank, rhs=rhs: e.matmul(
                        b[:, :], HQT[:, 128 * it:128 * (it + 1)], rhs,
                        start=(it == 0), stop=(it == 21)),
                        waits=w, inc=pe)
                dn_toks[n] = p.tok(pe)

        def hquant(c):
            w0 = (tok_fh2[0],)
            if last_q[0]:
                w0 = w0 + (last_q[0],)
            if c == 0:
                w0 = w0 + (P2END[1], P2END[2])
            dw = (htp_tok[0],) if htp_tok[0] else (P2END[0],)
            src3a = F_H[:, 0:1408].rearrange("p (b s) -> p b s", s=16)
            src3b = F_H[:, 1408:2816].rearrange("p (b s) -> p b s", s=16)
            p.emit("vector", lambda e: e.tensor_reduce(out=MBx[:, 0:88], in_=src3a,
                   axis=AX.X, op=OP.max, apply_absolute_value=True),
                   waits=w0, inc=dve)
            p.emit("vector", lambda e: e.tensor_scalar(
                   out=FAx[:, 0:1408], in0=F_H[:, 0:1408], scalar1=0.0,
                   scalar2=None, op0=OP.max), inc=dve)
            p.emit("vector", lambda e: e.tensor_scalar(
                   out=B1x[:, 0:1408], in0=F_H[:, 0:1408], scalar1=-1.0,
                   scalar2=0.0, op0=OP.mult, op1=OP.max), inc=dve)
            tok_abs0 = p.emit("vector", lambda e: e.tensor_tensor(
                   out=FAx[:, 0:1408],
                   in0=FAx[:, 0:1408], in1=B1x[:, 0:1408], op=OP.add), inc=dve)
            p.emit("vector", lambda e: e.tensor_reduce(out=MBx[:, 88:176],
                   in_=src3b,
                   axis=AX.X, op=OP.max, apply_absolute_value=True),
                   waits=(tok_poolFH[0],), inc=dve)
            p.emit("vector", lambda e: e.tensor_scalar_mul(T1x[:, 0:176],
                   MBx[:, 0:176], 0.5), inc=dve)
            p.emit("vector", lambda e: e.tensor_scalar_mul(T2x[:, 0:176],
                   T1x[:, 0:176], C3), inc=dve)
            p.emit("vector", lambda e: e.scalar_tensor_tensor(out=MBx[:, 0:176],
                   in0=T2x[:, 0:176], scalar=-2.0, in1=T1x[:, 0:176], op0=OP.mult,
                   op1=OP.add), inc=dve)
            p.emit("vector", lambda e: e.scalar_tensor_tensor(out=T1x[:, 0:176],
                   in0=T2x[:, 0:176], scalar=-1.0, in1=MBx[:, 0:176], op0=OP.mult,
                   op1=OP.add), inc=dve)
            p.emit("vector", lambda e: e.scalar_tensor_tensor(out=SBx[:, 0:176],
                   in0=T1x[:, 0:176], scalar=C3, in1=T2x[:, 0:176], op0=OP.mult,
                   op1=OP.add), inc=dve)
            p.emit("vector", lambda e: e.tensor_scalar_max(SBx[:, 0:176],
                   SBx[:, 0:176], 1e-12), inc=dve)
            p.emit("vector", lambda e: e.reciprocal(out=T2x[:, 0:176],
                   in_=SBx[:, 0:176]), inc=dve)
            p.emit("vector", lambda e: e.tensor_copy(out=SBB[:, 0:176],
                   in_=T2x[:, 0:176]), inc=dve)
            p.emit("vector", lambda e: e.tensor_copy(out=SBB[:, 176:352],
                   in_=SBx[:, 0:176]), inc=dve)
            FA3 = FAx[:, 0:2816].rearrange("p (b s) -> p b s", s=16)
            p.emit("vector", lambda e: e.tensor_scalar(out=FAx[:, 1408:2816],
                   in0=F_H[:, 1408:2816], scalar1=0.0, scalar2=None, op0=OP.max),
                   inc=dve)
            p.emit("vector", lambda e: e.tensor_scalar(out=B1x[:, 1408:2816],
                   in0=F_H[:, 1408:2816], scalar1=-1.0, scalar2=0.0, op0=OP.mult,
                   op1=OP.max), inc=dve)
            p.emit("vector", lambda e: e.tensor_tensor(out=FAx[:, 1408:2816],
                   in0=FAx[:, 1408:2816], in1=B1x[:, 1408:2816], op=OP.add),
                   inc=dve)
            rb = SBB[:, 0:176].unsqueeze(2).broadcast_to((128, 176, 16))
            p.emit("vector", lambda e, rb=rb: e.tensor_tensor(out=FA3, in0=FA3,
                   in1=rb, op=OP.mult), inc=dve)

            def hcmp(i, dst):
                return lambda e: e.tensor_single_scalar(out=dst,
                       in_=FAx[:, 0:2816], scalar=THR[i], op=OP.is_gt)

            p.emit("vector", hcmp(0, B1x[:, 0:2816]), inc=dve)
            for i in (1, 2, 3):
                p.emit("vector", hcmp(i, B2x[:, 0:2816]), inc=dve)
                p.emit("vector", lambda e: e.tensor_tensor(out=B1x[:, 0:2816],
                       in0=B1x[:, 0:2816], in1=B2x[:, 0:2816], op=OP.add),
                       inc=dve)
            p.emit("vector", lambda e: e.tensor_scalar_mul(B1x[:, 0:2816],
                   B1x[:, 0:2816], 0.5), inc=dve)
            for i in (4, 5):
                p.emit("vector", hcmp(i, B2x[:, 0:2816]), inc=dve)
                p.emit("vector", lambda e: e.tensor_tensor(out=B1x[:, 0:2816],
                       in0=B1x[:, 0:2816], in1=B2x[:, 0:2816], op=OP.add),
                       inc=dve)
            p.emit("vector", hcmp(6, B2x[:, 0:2816]), inc=dve)
            p.emit("vector", lambda e: e.scalar_tensor_tensor(out=B1x[:, 0:2816],
                   in0=B2x[:, 0:2816], scalar=2.0, in1=B1x[:, 0:2816],
                   op0=OP.mult, op1=OP.add), inc=dve)
            p.emit("vector", lambda e: e.tensor_single_scalar(out=FAx[:, 0:2816],
                   in_=F_H[:, 0:2816], scalar=0.0, op=OP.is_gt), inc=dve)
            p.emit("vector", lambda e: e.tensor_scalar(out=FAx[:, 0:2816],
                   in0=FAx[:, 0:2816], scalar1=2.0, scalar2=-1.0, op0=OP.mult,
                   op1=OP.add), inc=dve)
            sb_b = SBB[:, 176:352].unsqueeze(2).broadcast_to((128, 176, 16))
            p.emit("vector", lambda e, sb_b=sb_b: e.tensor_tensor(
                   out=B2x[:, 0:2816].rearrange("p (b s) -> p b s", s=16),
                   in0=B1x[:, 0:2816].rearrange("p (b s) -> p b s", s=16),
                   in1=sb_b, op=OP.mult), inc=dve)
            dw = (htp_tok[0],) if htp_tok[0] else (P2END[0],)
            tok_d = p.emit("vector", lambda e: e.tensor_tensor(
                   out=HQ[:, 0:2816], in0=B2x[:, 0:2816],
                   in1=FAx[:, 0:2816], op=OP.mult), waits=dw, inc=dve)
            last_q[0] = tok_d
            hq_toks[c] = (tok_d,)

        def evac_ship(c):
            for n in range(2):
                w = (dn_toks[n], "out")
                if c == 0:
                    w = w + P2END
                ev_toks[n] = p.emit("scalar", lambda e, n=n: e.copy(
                    out=EV[:, 512 * n:512 * (n + 1)], in_=(S0, S1)[n][:, :]),
                    waits=w, inc=act)
                p.emit("gpsimd", lambda e, c=c, n=n: e.dma_start(
                    out=d_y[128 * c:128 * (c + 1), 512 * n:512 * (n + 1)],
                    in_=EV[:, 512 * n:512 * (n + 1)],
                    accum_op=OP.add), waits=(ev_toks[n],), inc="out", k=16)

        xquant_dve(0)
        xq_tposes(0)
        for c in range(NCH):
            xload(c + 1)
            gates_silu(c)
            if c + 1 < NCH:
                xquant_dve(c + 1)
                xq_tposes(c + 1)
            if c >= 1:
                tposes_down(c - 1)
            hquant(c)
            if c >= 1:
                evac_ship(c - 1)
        tposes_down(NCH - 1)
        evac_ship(NCH - 1)

        p.emit("gpsimd", lambda e: e.engine_nop(), waits=("out",))
        p.flush(block)
        s3.close()

    return nc


def kernel(L0_attn_x, L0_attn_Wqkv, L0_attn_bqkv, L0_attn_Wout, L0_attn_bout,
           L0_attn_gamma, L0_attn_beta, L0_ffn_w_gate, L0_ffn_w_up, L0_ffn_w_down):
    f32 = np.float32
    x = np.ascontiguousarray(np.asarray(L0_attn_x, f32).reshape(8 * T, D))
    wqkvT = np.ascontiguousarray(np.asarray(L0_attn_Wqkv, f32).T)
    bqkv = np.asarray(L0_attn_bqkv, f32)
    bqk_cols = np.ascontiguousarray(bqkv[:2048].reshape(16, 128).T)
    bqkk_row = np.ascontiguousarray(bqkv[1024:2048].reshape(1, D))
    bqv_row = np.ascontiguousarray(bqkv[2048:].reshape(1, D))
    woutT = np.ascontiguousarray(np.asarray(L0_attn_Wout, f32).T)
    bo_row = np.ascontiguousarray(np.asarray(L0_attn_bout, f32).reshape(1, D))
    gamma_t = np.ascontiguousarray(np.broadcast_to(
        np.asarray(L0_attn_gamma, f32).reshape(1, D), (128, D)))
    beta_t = np.ascontiguousarray(np.broadcast_to(
        np.asarray(L0_attn_beta, f32).reshape(1, D), (128, D)))
    wg = np.ascontiguousarray(np.asarray(L0_ffn_w_gate, f32))
    wu = np.ascontiguousarray(np.asarray(L0_ffn_w_up, f32))
    wd = np.ascontiguousarray(np.asarray(L0_ffn_w_down, f32))
    ident = np.eye(128, dtype=ml_dtypes.bfloat16)
    tri = np.triu(np.ones((128, 128), np.float32))
    mtri = np.ascontiguousarray(np.tile(tri, (1, 4))).astype(ml_dtypes.bfloat16)
    onesf = np.ones((1, 128), f32)
    zeros = np.zeros((T, D), f32)

    shared = dict(wqkvT=wqkvT, bqk_cols=bqk_cols, bqkk_row=bqkk_row,
                  bqv_row=bqv_row, woutT=woutT, bo_row=bo_row,
                  gamma_t=gamma_t, beta_t=beta_t, wg=wg, wu=wu, wd=wd,
                  ident=ident, mtri=mtri, onesf=onesf)
    in_maps = []
    for c in range(8):
        d = dict(shared)
        d["xm"] = np.ascontiguousarray(x[c * T:(c + 1) * T])
        if c % 2 == 1:
            d["xp"] = np.ascontiguousarray(x[(c - 1) * T:c * T])
            d["mc"] = np.full((128, 1), 1.0, f32)
        else:
            d["xp"] = zeros
            d["mc"] = np.zeros((128, 1), f32)
        in_maps.append(d)

    nc = build_program()
    res = run_bass_kernel_spmd(nc, in_maps, core_ids=list(range(8)))
    out = np.concatenate([np.asarray(r["y"], np.float32) for r in res.results], axis=0)
    return out.reshape(4, 4096, D)
